# version 05
# baseline (speedup 1.0000x reference)
import numpy as np
from contextlib import ExitStack

import concourse.bass as bass
import concourse.mybir as mybir
from concourse.bass_utils import run_bass_kernel_spmd

F32 = mybir.dt.float32
BF16 = mybir.dt.bfloat16
AF = mybir.ActivationFunctionType
ALU = mybir.AluOpType
AX = mybir.AxisListType

NCORES = 8
S = 2048
H = 2048
KS = 16
INTER = 4096
JPAD = 8576
NJB = 67
HEADS = 32
CHUNK = 256
NCH = 8
NTB = 16
NE = 32
NI = 768
NIB = 6
HALF = 1024
EPS = 1e-5

CONST_VALS = [0.0, 1.0, -1.0, -2.0, 2.0, 0.5, -0.5, 100.0, EPS,
              1.0 / 2048, -1.0 / 2048, 1.0 / 4096]

ENGS = ("sync", "tensor", "vector", "scalar", "gpsimd")


class Rsrc:
    __slots__ = ("w", "rs", "name")

    def __init__(self, name=""):
        self.w = None
        self.rs = []
        self.name = name


class Prog:

    def __init__(self):
        self.ops = {e: [] for e in ENGS}
        self.marked = {e: [] for e in ENGS}
        self.pending = {}

    def op(self, eng, fn, reads=(), writes=(), deps=(), dma=False,
           sync_after=False):
        dl = list(deps)
        dl.extend(self.pending.pop(eng, ()))
        for r in reads:
            if r.w is not None:
                dl.append(r.w)
        for r in writes:
            if r.w is not None:
                dl.append(r.w)
            dl.extend(r.rs)
        idx = len(self.ops[eng])
        tok = (eng, idx)
        seen = set()
        fl = []
        for d in dl:
            if d is None or d in seen:
                continue
            seen.add(d)
            if d[0] == eng:
                continue
            fl.append(d)
        self.ops[eng].append((fn, fl, dma, sync_after))
        self.marked[eng].append(dma or sync_after)
        for d in fl:
            self.marked[d[0]][d[1]] = True
        for r in reads:
            r.rs.append(tok)
        for r in writes:
            r.w = tok
            r.rs = []
        return tok

    def barrier(self):
        toks = []
        for e in ENGS:
            n = len(self.ops[e])
            if n:
                toks.append((e, n - 1))
        return toks

    def fence(self):
        toks = self.barrier()
        for e in ENGS:
            self.pending.setdefault(e, []).extend(toks)

    def emit(self, nc, block, stack):
        sems = {e: stack.enter_context(nc.semaphore(f"esem_{e}")) for e in ENGS}
        vals = {}
        for e in ENGS:
            acc = 0
            va = []
            for i, m in enumerate(self.marked[e]):
                if m:
                    acc += 16 if self.ops[e][i][2] else 1
                va.append(acc)
            vals[e] = va

        def section(engname):
            def body(eng):
                memo = {}
                for i, (fn, deps, is_dma, sync_after) in enumerate(
                        self.ops[engname]):
                    need = {}
                    for (de, di) in deps:
                        v = vals[de][di]
                        if v > need.get(de, 0):
                            need[de] = v
                    for de, v in need.items():
                        if memo.get(de, 0) >= v:
                            continue
                        eng.wait_ge(sems[de], v)
                        memo[de] = v
                    inst = fn(eng)
                    if self.marked[engname][i]:
                        inst.then_inc(sems[engname], 16 if is_dma else 1)
                    if sync_after:
                        eng.wait_ge(sems[engname], vals[engname][i])
                if engname == "sync" and vals["sync"]:
                    eng.wait_ge(sems["sync"], vals["sync"][-1])
            return body

        block.sync(section("sync"))
        block.tensor(section("tensor"))
        block.vector(section("vector"))
        block.scalar(section("scalar"))
        block.gpsimd(section("gpsimd"))


def build_nc(debug=False):
    nc = bass.Bass("TRN2", target_bir_lowering=False)

    x_d = nc.dram_tensor("x", [S, H], F32, kind="ExternalInput")
    wip_d = nc.dram_tensor("wip", [NJB, KS, 128, 128], F32, kind="ExternalInput")
    wout_d = nc.dram_tensor("wout", [32, 128, H], F32, kind="ExternalInput")
    gw_d = nc.dram_tensor("gw", [NE, KS, 128, NI], F32, kind="ExternalInput")
    uw_d = nc.dram_tensor("uw", [NE, KS, 128, NI], F32, kind="ExternalInput")
    dw_d = nc.dram_tensor("dw", [NE, NIB, 128, H], F32, kind="ExternalInput")
    rwt_d = nc.dram_tensor("rwt", [128, KS, NE], F32, kind="ExternalInput")
    convw_d = nc.dram_tensor("convw", [128, 34, 4], F32, kind="ExternalInput")
    convb_d = nc.dram_tensor("convb", [128, 34], F32, kind="ExternalInput")
    nw_d = nc.dram_tensor("nw", [128, 32], F32, kind="ExternalInput")
    vecs_d = nc.dram_tensor("vecs", [32, 3], F32, kind="ExternalInput")
    consts_d = nc.dram_tensor("consts", [128, len(CONST_VALS)], F32,
                              kind="ExternalInput")
    tri_d = nc.dram_tensor("tri", [128, 512], F32, kind="ExternalInput")
    ident_d = nc.dram_tensor("ident", [128, 128], F32, kind="ExternalInput")
    ones1_d = nc.dram_tensor("ones1", [1, 128], F32, kind="ExternalInput")
    onescol_d = nc.dram_tensor("onescol", [128, 1], F32, kind="ExternalInput")

    out_d = nc.dram_tensor("out", [S, H], F32, kind="ExternalOutput")
    SCR = "ExternalOutput" if debug else "Internal"
    xn_d = nc.dram_tensor("xn_scr", [128, KS, S], BF16, kind=SCR)
    gate_d = nc.dram_tensor("gate_scr", [128, 32, S], BF16, kind=SCR)
    hbc_d = nc.dram_tensor("hbc_scr", [128, 34, S + 3], BF16, kind=SCR)
    res_d = nc.dram_tensor("res_scr", [128, NTB, H], F32, kind=SCR)
    woutb_d = nc.dram_tensor("woutb_scr", [128, 32, H], BF16, kind="Internal")
    acrow_d = nc.dram_tensor("acrow_scr", [NCH, 32, 256], F32, kind="Internal")
    moeg_d = nc.dram_tensor("moeg_scr", [NE, 128, KS * NI], BF16, kind="Internal")
    moeu_d = nc.dram_tensor("moeu_scr", [NE, 128, KS * NI], BF16, kind="Internal")
    moed_d = nc.dram_tensor("moed_scr", [NE, 128, NIB * H], BF16, kind="Internal")
    dbgv_d = nc.dram_tensor("dbgv_scr", [128, 4096], F32,
                            kind="ExternalOutput") if debug else None

    P = Prog()
    R = Rsrc

    stack = ExitStack()
    with stack:
        def sbt(st, name, shape, dtype=F32):
            return st.enter_context(nc.sbuf_tensor(name, shape, dtype))

        pb = [nc.alloc_psum_tensor(f"pb{i}", [128, 512], F32) for i in range(8)]
        pbr = [R(f"pb{i}") for i in range(8)]

        consts_sb = sbt(stack, "consts_sb", [128, len(CONST_VALS)])
        for ci, cv in enumerate(CONST_VALS):
            nc.const_aps.register(F32, cv, consts_sb[:, ci:ci + 1])
        identf = sbt(stack, "identf", [128, 128])
        identb = sbt(stack, "identb", [128, 128], BF16)
        ones1_sb = sbt(stack, "ones1_sb", [1, 128])
        onescol_bf = sbt(stack, "onescol_bf", [128, 1], BF16)
        vecs_sb = sbt(stack, "vecs_sb", [32, 3])
        acol_sb = sbt(stack, "acol_sb", [32, 1])
        w_all = sbt(stack, "w_all", [128, NTB * 32])
        ldbg = sbt(stack, "ldbg", [128, 96], BF16) if debug else None
        rldbg = R("ldbg")
        p12 = ExitStack()
        tri_sb = sbt(p12, "tri_sb", [128, 512])
        convw_sb = sbt(p12, "convw_sb", [128, 34 * 4])
        convb_sb = sbt(p12, "convb_sb", [128, 34])
        nw_sb = sbt(p12, "nw_sb", [128, 32])
        dt_fm = sbt(p12, "dt_fm", [32, S])
        acum_fm = sbt(p12, "acum_fm", [32, S])
        dt_tm = sbt(p12, "dt_tm", [128, NTB * 32])
        acum_tm = sbt(p12, "acum_tm", [128, NTB * 32])
        ones32 = sbt(p12, "ones32", [32, CHUNK])
        zero32 = sbt(p12, "zero32", [32, 1])
        zpad = sbt(p12, "zpad", [128, 102], BF16)
        rwt_bf = sbt(p12, "rwt_bf", [128, KS * 32], BF16)

        r_wall = R("wall")
        r_dt = R("dtgrp")
        r_acol = R("acol")
        r_zpad = R("zpad")
        rmoeg = [R(f"moeg{e}") for e in range(NE)]
        rmoeu = [R(f"moeu{e}") for e in range(NE)]
        rmoed = [R(f"moed{e}") for e in range(NE)]
        rwoutb = R("woutb")
        rxnd = R("xn_dram")
        rgated = R("gate_dram")
        rhbcd = R("hbc_dram")
        rresd = [R(f"res_d{i}") for i in range(NTB)]

        for d, s in [
            (consts_sb[:, :], consts_d.ap()),
            (tri_sb[:, :], tri_d.ap()),
            (identf[:, :], ident_d.ap()),
            (ones1_sb[:, :], ones1_d.ap()),
            (convw_sb[:, :], convw_d.ap().rearrange("p a b -> p (a b)")),
            (convb_sb[:, :], convb_d.ap()),
            (nw_sb[:, :], nw_d.ap()),
            (vecs_sb[:, :], vecs_d.ap()),
        ]:
            P.op("sync", (lambda d_, s_: lambda en: en.dma_start(d_, s_))(d, s),
                 dma=True)
        P.op("gpsimd", lambda en: en.dma_start(identb[:, :], ident_d.ap()), dma=True)
        P.op("gpsimd", lambda en: en.dma_start(onescol_bf[:, :], onescol_d.ap()),
             dma=True)
        P.op("gpsimd", lambda en: en.dma_start(
            rwt_bf[:, :], rwt_d.ap().rearrange("p a b -> p (a b)")), dma=True)
        P.fence()
        tA = P.op("scalar", lambda en: en.activation(acol_sb[:, :], vecs_sb[:, 0:1],
                                                     AF.Exp), writes=[r_acol])
        P.op("vector", lambda en: en.tensor_scalar(acol_sb[:, :], acol_sb[:, :],
                                                   -1.0, None, ALU.mult),
             reads=[r_acol], writes=[r_acol])
        P.op("vector", lambda en: en.memset(ones32[:, :], 1.0))
        P.op("vector", lambda en: en.memset(zero32[:, :], 0.0))
        P.op("vector", lambda en: en.memset(zpad[:, :], 0.0), writes=[r_zpad])
        P.op("sync", lambda en: en.dma_start(
            hbc_d.ap()[:, :, 0:3], zpad[:, :].rearrange("p (a b) -> p a b", a=34)),
            reads=[r_zpad], writes=[rhbcd], dma=True)

        p01 = ExitStack()
        xn_fm = sbt(p01, "xn_fm", [128, KS * 2048], BF16)
        r_xnfm = R("xn_fm")

        p0 = ExitStack()
        xblk = [sbt(p0, "xblk0", [128, H]), sbt(p0, "xblk1", [128, H])]
        rxblk = [R("xblk0"), R("xblk1")]
        xsc = sbt(p0, "xsc", [128, H])
        sqjunk = sbt(p0, "sqjunk", [128, H])
        stat = sbt(p0, "stat", [128, 8])
        xn_tmb = sbt(p0, "xn_tmb", [128, H], BF16)
        prout = sbt(p0, "prout", [128, 128])
        routc = sbt(p0, "routc", [128, 4])
        rxsc = R("xsc"); rstat = R("stat"); rxntm = R("xn_tmb")
        rprout = R("prout"); rroutc = R("routc")

        for tb in range(NTB):
            xb = xblk[tb % 2]
            rxb = rxblk[tb % 2]
            P.op("sync", (lambda xb_, tb_: lambda en: en.dma_start(
                xb_[:, :], x_d.ap()[tb_ * 128:(tb_ + 1) * 128, :]))(xb, tb),
                writes=[rxb], dma=True)
            P.op("vector", (lambda xb_: lambda en: en.reduce_sum(
                stat[:, 0:1], xb_[:, :], AX.X))(xb),
                reads=[rxb], writes=[rstat], sync_after=True)
            P.op("vector", lambda en: en.tensor_scalar(
                stat[:, 1:2], stat[:, 0:1], -1.0 / 2048, None, ALU.mult),
                reads=[rstat], writes=[rstat])
            P.op("scalar", (lambda xb_: lambda en: en.activation(
                xsc[:, :], xb_[:, :], AF.Identity, bias=stat[:, 1:2]))(xb),
                reads=[rxb, rstat], writes=[rxsc])
            P.op("scalar", lambda en: en.activation(
                sqjunk[:, :], xsc[:, :], AF.Square, accum_out=stat[:, 2:3]),
                reads=[rxsc], writes=[rstat])
            P.op("vector", lambda en: en.tensor_scalar(
                stat[:, 3:4], stat[:, 2:3], 1.0 / 2048, EPS, ALU.mult, ALU.add),
                reads=[rstat], writes=[rstat])
            P.op("scalar", lambda en: en.activation(
                stat[:, 4:5], stat[:, 3:4], AF.Ln),
                reads=[rstat], writes=[rstat])
            P.op("scalar", lambda en: en.activation(
                stat[:, 5:6], stat[:, 4:5], AF.Exp, scale=-0.5),
                reads=[rstat], writes=[rstat])
            P.op("vector", lambda en: en.tensor_tensor(
                stat[:, 6:7], stat[:, 1:2], stat[:, 5:6], ALU.mult),
                reads=[rstat], writes=[rstat])
            P.op("scalar", (lambda xb_: lambda en: en.activation(
                xn_tmb[:, :], xb_[:, :], AF.Identity, scale=stat[:, 5:6],
                bias=stat[:, 6:7]))(xb),
                reads=[rxb, rstat], writes=[rxntm])
            for ksl in range(KS):
                bank = (tb * KS + ksl) % 2 + 6
                P.op("tensor", (lambda k_, b_: lambda en: en.transpose(
                    pb[b_].ap().bitcast(BF16)[:, 0:128],
                    xn_tmb[:, k_ * 128:(k_ + 1) * 128], identb[:, :]))(ksl, bank),
                    reads=[rxntm], writes=[pbr[bank]])
                P.op("vector", (lambda k_, b_, tb_: lambda en: en.tensor_copy(
                    xn_fm[:, k_ * 2048 + tb_ * 128:k_ * 2048 + (tb_ + 1) * 128],
                    pb[b_].ap().bitcast(BF16)[:, 0:128]))(ksl, bank, tb),
                    reads=[pbr[bank]], writes=[r_xnfm])
            lgt = None
            for ksl in range(KS):
                lgt = P.op("tensor", (lambda k_, tb_: lambda en: en.matmul(
                    pb[2][:, 0:32],
                    xn_fm[:, k_ * 2048 + tb_ * 128:k_ * 2048 + (tb_ + 1) * 128],
                    rwt_bf[:, k_ * 32:(k_ + 1) * 32],
                    start=(k_ == 0), stop=(k_ == KS - 1)))(ksl, tb),
                    reads=[r_xnfm],
                    writes=[pbr[2]] if ksl == 0 else (),
                    deps=() if lgt is None else [lgt])
            pbr[2].w = lgt
            if debug and tb == NTB - 1:
                P.op("vector", lambda en: en.tensor_copy(
                    ldbg[:, 0:32], pb[2][:, 0:32]), reads=[pbr[2]], writes=[rldbg])
            probs = prout[:, 0:32]; cur = prout[:, 32:64]
            selm = prout[:, 64:96]; psel = prout[:, 96:128]
            P.op("vector", lambda en: en.reduce_max(
                routc[:, 0:1], pb[2][:, 0:32], AX.X, negate=True),
                reads=[pbr[2]], writes=[rroutc], sync_after=True)
            P.op("scalar", lambda en: en.activation(
                prout[:, 0:32], pb[2][:, 0:32], AF.Exp, bias=routc[:, 0:1]),
                reads=[pbr[2], rroutc], writes=[rprout])
            if debug and tb == NTB - 1:
                P.op("vector", lambda en: en.tensor_copy(
                    ldbg[:, 32:64], prout[:, 0:32]), reads=[rprout], writes=[rldbg])
                P.op("vector", lambda en: en.tensor_copy(
                    ldbg[:, 64:65], routc[:, 0:1]), reads=[rroutc], writes=[rldbg])
            P.op("vector", lambda en: en.tensor_copy(prout[:, 32:64],
                                                     prout[:, 0:32]),
                 reads=[rprout], writes=[rprout])
            P.op("vector", lambda en: en.memset(prout[:, 64:96], 0.0),
                 writes=[rprout])
            for r4 in range(4):
                P.op("vector", lambda en: en.reduce_max(
                    routc[:, 3:4], prout[:, 32:64], AX.X),
                    reads=[rprout], writes=[rroutc], sync_after=True)
                P.op("vector", lambda en: en.tensor_tensor(
                    prout[:, 96:128], prout[:, 32:64],
                    routc[:, 3:4].broadcast_to([128, 32]), ALU.is_ge),
                    reads=[rprout, rroutc], writes=[rprout])
                P.op("vector", lambda en: en.tensor_tensor(
                    prout[:, 64:96], prout[:, 64:96], prout[:, 96:128], ALU.add),
                    reads=[rprout], writes=[rprout])
                P.op("vector", lambda en: en.scalar_tensor_tensor(
                    prout[:, 32:64], prout[:, 96:128], -2.0, prout[:, 32:64],
                    ALU.mult, ALU.add),
                    reads=[rprout], writes=[rprout])
            P.op("vector", lambda en: en.tensor_scalar(
                prout[:, 64:96], prout[:, 64:96], 1.0, None, ALU.min),
                reads=[rprout], writes=[rprout])
            P.op("vector", lambda en: en.tensor_tensor(
                prout[:, 96:128], prout[:, 0:32], prout[:, 64:96], ALU.mult),
                reads=[rprout], writes=[rprout])
            P.op("vector", lambda en: en.reduce_sum(
                routc[:, 1:2], prout[:, 96:128], AX.X),
                reads=[rprout], writes=[rroutc])
            P.op("scalar", lambda en: en.activation(
                routc[:, 2:3], routc[:, 1:2], AF.Ln),
                reads=[rroutc], writes=[rroutc])
            P.op("scalar", lambda en: en.activation(
                routc[:, 2:3], routc[:, 2:3], AF.Exp, scale=-1.0),
                reads=[rroutc], writes=[rroutc])
            P.op("vector", (lambda tb_: lambda en: en.tensor_tensor(
                w_all[:, tb_ * 32:(tb_ + 1) * 32], prout[:, 96:128],
                routc[:, 2:3].broadcast_to([128, 32]), ALU.mult))(tb),
                reads=[rprout, rroutc], writes=[r_wall])
        P.op("sync", lambda en: en.dma_start(
            xn_d.ap().rearrange("p a b -> p (a b)"), xn_fm[:, :]),
            reads=[r_xnfm], writes=[rxnd], dma=True)
        p0.close()
        P.fence()

        p1 = ExitStack()
        wipb = [sbt(p1, "wipb0", [128, KS * 128], BF16),
                sbt(p1, "wipb1", [128, KS * 128], BF16)]
        rwipb = [R("wipb0"), R("wipb1")]
        strip = [sbt(p1, "strip0", [128, S], BF16), sbt(p1, "strip1", [128, S], BF16)]
        rstrip = [R("strip0"), R("strip1")]
        woutc = [sbt(p1, "woutc0", [128, H], BF16), sbt(p1, "woutc1", [128, H], BF16)]
        rwoutc = [R("woutc0"), R("woutc1")]
        dtw = sbt(p1, "dtw", [32, S])
        rdtw = R("dtw")
        racrow = R("acrow")

        jorder = [66] + list(range(66))
        for it, jb in enumerate(jorder):
            wb = wipb[it % 2]; rwb = rwipb[it % 2]
            stp = strip[it % 2]; rstp = rstrip[it % 2]
            P.op("gpsimd", (lambda jb_, wb_: lambda en: en.dma_start(
                wb_[:, :].rearrange("p (a b) -> p a b", a=KS),
                wip_d.ap()[jb_].transpose([1, 0, 2])))(jb, wb),
                writes=[rwb], dma=True)
            for tt in range(4):
                bank = tt % 2
                mmt = None
                for ksl in range(KS):
                    mmt = P.op("tensor", (lambda k_, tt_, b_, wb_: lambda en:
                         en.matmul(
                             pb[b_][:, :],
                             wb_[:, k_ * 128:(k_ + 1) * 128],
                             xn_fm[:, k_ * 2048 + tt_ * 512:
                                   k_ * 2048 + (tt_ + 1) * 512],
                             start=(k_ == 0), stop=(k_ == KS - 1)))(
                                 ksl, tt, bank, wb),
                        reads=[rwb, r_xnfm] if ksl == 0 else [rwb],
                        writes=[pbr[bank]] if ksl == 0 else (),
                        deps=() if mmt is None else [mmt])
                pbr[bank].w = mmt
                if jb == 66:
                    P.op("scalar", (lambda tt_, b_: lambda en: en.copy(
                        dtw[:, tt_ * 512:(tt_ + 1) * 512], pb[b_][0:32, :]))(
                            tt, bank),
                        reads=[pbr[bank]], writes=[rdtw])
                elif jb < 32:
                    P.op("scalar", (lambda tt_, b_, stp_: lambda en: en.activation(
                        stp_[:, tt_ * 512:(tt_ + 1) * 512], pb[b_][:, :],
                        AF.Silu))(tt, bank, stp),
                        reads=[pbr[bank]], writes=[rstp])
                else:
                    P.op("scalar", (lambda tt_, b_, stp_: lambda en: en.copy(
                        stp_[:, tt_ * 512:(tt_ + 1) * 512], pb[b_][:, :]))(
                            tt, bank, stp),
                        reads=[pbr[bank]], writes=[rstp])
            if jb < 32:
                P.op("sync", (lambda jb_, stp_: lambda en: en.dma_start(
                    gate_d.ap()[:, jb_, :], stp_[:, :]))(jb, stp),
                    reads=[rstp], writes=[rgated] if jb == 31 else (), dma=True)
            elif jb < 66:
                P.op("sync", (lambda jb_, stp_: lambda en: en.dma_start(
                    hbc_d.ap()[:, jb_ - 32, 3:S + 3], stp_[:, :]))(jb, stp),
                    reads=[rstp], writes=[rhbcd] if jb == 65 else (), dma=True)
            else:
                P.op("vector", lambda en: en.tensor_scalar(
                    dtw[:, :], dtw[:, :], vecs_sb[:, 1:2], None, ALU.add),
                    reads=[rdtw], writes=[rdtw])
                P.op("scalar", lambda en: en.activation(
                    dt_fm[:, :], dtw[:, :], AF.Abs),
                    reads=[rdtw], writes=[r_dt])
                P.op("scalar", lambda en: en.activation(
                    dt_fm[:, :], dt_fm[:, :], AF.Exp, scale=-1.0),
                    reads=[r_dt], writes=[r_dt])
                P.op("scalar", lambda en: en.activation(
                    dt_fm[:, :], dt_fm[:, :], AF.Ln, bias=1.0),
                    reads=[r_dt], writes=[r_dt])
                P.op("vector", lambda en: en.tensor_scalar(
                    dtw[:, :], dtw[:, :], 0.0, None, ALU.max),
                    reads=[rdtw], writes=[rdtw])
                P.op("vector", lambda en: en.tensor_tensor(
                    dt_fm[:, :], dt_fm[:, :], dtw[:, :], ALU.add),
                    reads=[rdtw, r_dt], writes=[r_dt])
                P.op("vector", lambda en: en.tensor_scalar(
                    dt_fm[:, :], dt_fm[:, :], 100.0, None, ALU.min),
                    reads=[r_dt], writes=[r_dt])
                P.op("vector", lambda en: en.tensor_scalar(
                    dtw[:, :], dt_fm[:, :], acol_sb[:, :], None, ALU.mult),
                    reads=[r_dt, r_acol], writes=[rdtw])
                for c in range(NCH):
                    P.op("vector", (lambda c_: lambda en: en.tensor_tensor_scan(
                        acum_fm[:, c_ * 256:(c_ + 1) * 256], ones32[:, :],
                        dtw[:, c_ * 256:(c_ + 1) * 256], zero32[:, :],
                        ALU.mult, ALU.add))(c),
                        reads=[rdtw], writes=[r_dt])
                P.op("sync", lambda en: en.dma_start(
                    acrow_d.ap().transpose([1, 0, 2]),
                    acum_fm[:, :].rearrange("h (c l) -> h c l", c=NCH)),
                    reads=[r_dt], writes=[racrow], dma=True)
        for tb in range(NTB):
            for wi, (src, dst) in enumerate(((dt_fm, dt_tm), (acum_fm, acum_tm))):
                bank = 2 + (tb * 2 + wi) % 2
                P.op("tensor", (lambda tb_, s_, b_: lambda en: en.transpose(
                    pb[b_][0:128, 0:32], s_[:, tb_ * 128:(tb_ + 1) * 128],
                    identf[0:32, 0:32]))(tb, src, bank),
                    reads=[r_dt], writes=[pbr[bank]])
                P.op("vector", (lambda tb_, d_, b_: lambda en:
                     en.tensor_copy(d_[:, tb_ * 32:(tb_ + 1) * 32],
                                    pb[b_][0:128, 0:32]))(tb, dst, bank),
                    reads=[pbr[bank]], writes=[r_dt])
        for sl in range(32):
            wc = woutc[sl % 2]; rwc = rwoutc[sl % 2]
            P.op("gpsimd", (lambda sl_, wc_: lambda en: en.dma_start(
                wc_[:, :], wout_d.ap()[sl_]))(sl, wc),
                writes=[rwc], dma=True)
            P.op("vector", (lambda sl_, wc_: lambda en: en.tensor_scalar(
                wc_[:, :], wc_[:, :], nw_sb[:, sl_:sl_ + 1], None, ALU.mult))(
                    sl, wc),
                reads=[rwc], writes=[rwc])
            P.op("sync", (lambda sl_, wc_: lambda en: en.dma_start(
                woutb_d.ap()[:, sl_, :], wc_[:, :]))(sl, wc),
                reads=[rwc], writes=[rwoutb] if sl == 31 else (), dma=True)
        p1.close()
        p01.close()
        P.fence()

        p2 = ExitStack()
        hbc_ch = sbt(p2, "hbc_ch", [128, 34 * 259], BF16)
        rhbc = R("hbc_ch")
        cvt = sbt(p2, "cvt", [128, 256])
        rcvt = R("cvt")
        sg_ch = [sbt(p2, "sg_ch0", [128, 32 * 256], BF16)]
        rsg = [R("sg0")]
        xs_tm = [sbt(p2, "xs_tm0", [128, INTER], BF16),
                 sbt(p2, "xs_tm1", [128, INTER], BF16)]
        rxs = [R("xs0"), R("xs1")]
        xdt_a = [sbt(p2, "xdt0", [128, INTER], BF16),
                 sbt(p2, "xdt1", [128, INTER], BF16)]
        rxdt = [R("xdt0"), R("xdt1")]
        btm = sbt(p2, "btm", [128, 256], BF16)
        rbtm = R("btm")
        gtm_sb = sbt(p2, "gtm_sb", [128, 512], BF16)
        rgtm = R("gtm")
        ltu = [sbt(p2, f"ltu{i}", [128, 512]) for i in range(2)]
        rltu = [R("ltu0"), R("ltu1")]
        ltb = [sbt(p2, f"ltb{i}", [128, 512], BF16) for i in range(2)]
        rltb = [R("ltb0"), R("ltb1")]
        mt_sb = [sbt(p2, f"mt_sb{i}", [128, 512], BF16) for i in range(2)]
        rmt = [R("mt0"), R("mt1")]
        xdd = [sbt(p2, f"xdd{i}", [128, 256], BF16) for i in range(2)]
        rxdd = [R("xdd0"), R("xdd1")]
        t1_sb = [sbt(p2, f"t1_sb{i}", [128, 256]) for i in range(2)]
        rt1 = [R("t10"), R("t11")]
        tsum_s = sbt(p2, "tsum_s", [128, 256])
        tsum = [tsum_s, tsum_s]
        rtsum_s = R("ts")
        rtsum = [rtsum_s, rtsum_s]
        ybf = [sbt(p2, f"ybf{i}", [128, 256], BF16) for i in range(2)]
        rybf = [R("ybf0"), R("ybf1")]
        st_f = sbt(p2, "st_f", [128, HEADS * 128])
        rst = [R(f"st{h}") for h in range(HEADS)]
        st_bf = sbt(p2, "st_bf", [128, HEADS * 128], BF16)
        rstbf = [R(f"stbf{h}") for h in range(HEADS)]
        sdec = sbt(p2, "sdec", [128, 128])
        rsdec = R("sdec")
        y_fm = sbt(p2, "y_fm", [128, 32 * 256], BF16)
        ryfm = [R(f"yfm{h}") for h in range(HEADS)]
        yg2 = [sbt(p2, "yg20", [128, 256], BF16), sbt(p2, "yg21", [128, 256], BF16)]
        ryg2 = [R("yg20"), R("yg21")]
        ds_fm = sbt(p2, "ds_fm", [32, 256])
        rdsfm = R("dsfm")
        ds_tm = sbt(p2, "ds_tm", [128, 64])
        rdstm = R("dstm")
        sdx_tm = sbt(p2, "sdx_tm", [128, 64])
        rsdx = R("sdx")
        ecs = sbt(p2, "ecs", [32, 1])
        recs = R("ecs")
        ecsr = sbt(p2, "ecsr", [1, 32])
        recsr = R("ecsr")
        dbc = sbt(p2, "dbc", [128, 32])
        rdbc = R("dbc")
        dcol = sbt(p2, "dcol", [128, 32])
        rdcol = R("dcol")
        xsd = [sbt(p2, f"xsd{i}", [128, 256]) for i in range(2)]
        rxsd = [R("xsd0"), R("xsd1")]
        rowbuf = [sbt(p2, f"rowbuf{i}", [1, 512]) for i in range(2)]
        rrowb = [R("rowb0"), R("rowb1")]
        resst_t = tsum_s
        rresq = rtsum_s
        wo_t = [sbt(p2, "wo_t0", [128, 32 * 256], BF16),
                sbt(p2, "wo_t1", [128, 32 * 256], BF16)]
        rwot = [R("wot0"), R("wot1")]
        var_sb = sbt(p2, "var_sb", [128, 4])
        rvar = R("var")
        rstd_tm = sbt(p2, "rstd_tm", [128, 2])
        rrstd = R("rstd")
        xres = [sbt(p2, "xres0", [128, H]), sbt(p2, "xres1", [128, H])]
        rxres = [R("xres0"), R("xres1")]

        P.op("tensor", lambda en: en.transpose(
            pb[2][0:1, 0:32], vecs_sb[:, 2:3], identf[0:32, 0:32]),
            writes=[pbr[2]])
        P.op("vector", lambda en: en.tensor_copy(ecsr[:, :], pb[2][0:1, 0:32]),
             reads=[pbr[2]], writes=[recsr])
        P.op("tensor", lambda en: en.matmul(
            pb[3][:, 0:32], ones1_sb[:, :], ecsr[:, :], start=True, stop=True),
            reads=[recsr], writes=[pbr[3]])
        P.op("vector", lambda en: en.tensor_copy(dcol[:, :], pb[3][:, 0:32]),
             reads=[pbr[3]], writes=[rdcol])

        for c in range(NCH):
            sgb = sg_ch[0]; rsgb = rsg[0]
            P.op("sync", (lambda c_: lambda en: en.dma_start(
                hbc_ch[:, :].rearrange("p (a b) -> p a b", a=34),
                hbc_d.ap()[:, :, c_ * 256:c_ * 256 + 259]))(c),
                reads=[rhbcd], writes=[rhbc], dma=True)
            P.op("sync", (lambda c_, sgb_: lambda en: en.dma_start(
                sgb_[:, :].rearrange("p (a b) -> p a b", a=32),
                gate_d.ap()[:, :, c_ * 256:(c_ + 1) * 256]))(c, sgb),
                reads=[rgated], writes=[rsgb], dma=True)
            for tb2 in range(2):
                P.op("sync", (lambda c_, t_: lambda en: en.dma_start(
                    xres[t_][:, :],
                    x_d.ap()[c_ * 256 + t_ * 128:c_ * 256 + (t_ + 1) * 128, :]))(
                        c, tb2),
                    writes=[rxres[tb2]], dma=True)
            for sl in [32, 33] + list(range(32)):
                P.op("vector", (lambda sl_: lambda en: en.tensor_scalar(
                    cvt[:, :], hbc_ch[:, sl_ * 259:sl_ * 259 + 256],
                    convw_sb[:, sl_ * 4:sl_ * 4 + 1], None, ALU.mult))(sl),
                    reads=[rhbc], writes=[rcvt])
                for k in range(1, 4):
                    P.op("vector", (lambda sl_, k_: lambda en:
                         en.scalar_tensor_tensor(
                             cvt[:, :],
                             hbc_ch[:, sl_ * 259 + k_:sl_ * 259 + k_ + 256],
                             convw_sb[:, sl_ * 4 + k_:sl_ * 4 + k_ + 1],
                             cvt[:, :], ALU.mult, ALU.add))(sl, k),
                        reads=[rhbc], writes=[rcvt])
                P.op("scalar", (lambda sl_: lambda en: en.activation(
                    hbc_ch[:, sl_ * 259:sl_ * 259 + 256], cvt[:, :], AF.Silu,
                    bias=convb_sb[:, sl_:sl_ + 1]))(sl),
                    reads=[rcvt], writes=[rhbc])
            gtok = None
            for sbk in range(2):
                gtok = P.op("tensor", (lambda s_: lambda en: en.matmul(
                    pb[1][:, s_ * 256:(s_ + 1) * 256],
                    hbc_ch[:, 32 * 259 + s_ * 128:32 * 259 + (s_ + 1) * 128],
                    hbc_ch[:, 33 * 259:33 * 259 + 256], start=True, stop=True))(sbk),
                    reads=[rhbc], writes=[pbr[1]] if sbk == 0 else (),
                    deps=() if gtok is None else [gtok])
            pbr[1].w = gtok
            P.op("vector", lambda en: en.scalar_tensor_tensor(
                gtm_sb[:, :], pb[1][:, :], 1.0, tri_sb[:, :], ALU.mult, ALU.mult),
                reads=[pbr[1]], writes=[rgtm])
            for sl in range(32):
                for hf in range(2):
                    bank = 6 + (sl * 2 + hf) % 2
                    P.op("tensor", (lambda sl_, h_, b_: lambda en: en.transpose(
                        pb[b_].ap().bitcast(BF16)[:, 0:128],
                        hbc_ch[:, sl_ * 259 + h_ * 128:sl_ * 259 + (h_ + 1) * 128],
                        identb[:, :]))(sl, hf, bank),
                        reads=[rhbc], writes=[pbr[bank]])
                    P.op("vector", (lambda sl_, h_, b_: lambda en: en.tensor_copy(
                        xs_tm[h_][:, sl_ * 128:(sl_ + 1) * 128],
                        pb[b_].ap().bitcast(BF16)[:, 0:128]))(sl, hf, bank),
                        reads=[pbr[bank]], writes=[rxs[hf]])
            for hf in range(2):
                bank = 6 + hf
                P.op("tensor", (lambda h_, b_: lambda en: en.transpose(
                    pb[b_].ap().bitcast(BF16)[:, 0:128],
                    hbc_ch[:, 32 * 259 + h_ * 128:32 * 259 + (h_ + 1) * 128],
                    identb[:, :]))(hf, bank),
                    reads=[rhbc], writes=[pbr[bank]])
                P.op("vector", (lambda h_, b_: lambda en: en.tensor_copy(
                    btm[:, h_ * 128:(h_ + 1) * 128],
                    pb[b_].ap().bitcast(BF16)[:, 0:128]))(hf, bank),
                    reads=[pbr[bank]], writes=[rbtm])
            P.op("scalar", (lambda c_: lambda en: en.activation(
                ds_fm[:, :], acum_fm[:, c_ * 256:(c_ + 1) * 256], AF.Exp,
                scale=-1.0, bias=acum_fm[:, c_ * 256 + 255:c_ * 256 + 256]))(c),
                reads=[r_dt], writes=[rdsfm])
            for hf in range(2):
                bank = 6 + hf
                P.op("tensor", (lambda h_, b_: lambda en: en.transpose(
                    pb[b_][0:128, 0:32], ds_fm[:, h_ * 128:(h_ + 1) * 128],
                    identf[0:32, 0:32]))(hf, bank),
                    reads=[rdsfm], writes=[pbr[bank]])
                P.op("vector", (lambda h_, b_: lambda en: en.tensor_copy(
                    ds_tm[:, h_ * 32:(h_ + 1) * 32], pb[b_][0:128, 0:32]))(
                        hf, bank),
                    reads=[pbr[bank]], writes=[rdstm])
            P.op("scalar", (lambda c_: lambda en: en.activation(
                sdx_tm[:, 0:64], acum_tm[:, c_ * 64:(c_ + 1) * 64], AF.Exp))(c),
                reads=[r_dt], writes=[rsdx])
            for hf in range(2):
                P.op("vector", (lambda c_, h_: lambda en: en.tensor_tensor(
                    xdt_a[h_][:, :].rearrange("p (a b) -> p a b", a=32),
                    xs_tm[h_][:, :].rearrange("p (a b) -> p a b", a=32),
                    dt_tm[:, (2 * c_ + h_) * 32:(2 * c_ + h_ + 1) * 32]
                    .unsqueeze(2).broadcast_to([128, 32, 128]),
                    ALU.mult))(c, hf),
                    reads=[rxs[hf], r_dt], writes=[rxdt[hf]])
            if c > 0:
                P.op("scalar", (lambda c_: lambda en: en.activation(
                    ecs[:, :], acum_fm[:, c_ * 256 + 255:c_ * 256 + 256], AF.Exp))(c),
                    reads=[r_dt], writes=[recs])
                P.op("tensor", lambda en: en.transpose(
                    pb[0][0:1, 0:32], ecs[:, :], identf[0:32, 0:32]),
                    reads=[recs], writes=[pbr[0]])
                P.op("vector", lambda en: en.tensor_copy(
                    ecsr[:, :], pb[0][0:1, 0:32]),
                    reads=[pbr[0]], writes=[recsr])
                P.op("tensor", lambda en: en.matmul(
                    pb[0][:, 0:32], ones1_sb[:, :], ecsr[:, :], start=True,
                    stop=True), reads=[recsr], writes=[pbr[0]])
                P.op("vector", lambda en: en.tensor_copy(dbc[:, :],
                                                         pb[0][:, 0:32]),
                     reads=[pbr[0]], writes=[rdbc])

            def row_dma(hg):
                P.op("sync", (lambda c_, hg_: lambda en: en.dma_start(
                    rowbuf[hg_ % 2][:, :],
                    acrow_d.ap()[c_, hg_ * 2:(hg_ + 1) * 2, :]
                    .rearrange("a b -> (a b)").unsqueeze(0)))(c, hg),
                    reads=[racrow], writes=[rrowb[hg % 2]], dma=True)
            row_dma(0)
            row_dma(1)
            for h in range(HEADS):
                par = h % 2
                bA = 3 * par
                bB = 1 + 3 * par
                bC = 2 + 3 * par
                hg = h // 2
                P.op("tensor", (lambda h_, b_, hg_: lambda en: en.matmul(
                    pb[b_][:, 0:256], ones1_sb[:, :],
                    rowbuf[hg_ % 2][0:1, (h_ % 2) * 256:(h_ % 2 + 1) * 256],
                    start=True, stop=True))(h, bA, hg),
                    reads=[rrowb[hg % 2]], writes=[pbr[bA]])
                for sbk in range(2):
                    P.op("scalar", (lambda c_, h_, s_, b_: lambda en: en.activation(
                        ltu[h_ % 2][:, s_ * 256:(s_ + 1) * 256], pb[b_][:, 0:256],
                        AF.Relu, scale=-1.0,
                        bias=acum_tm[:, (2 * c_ + s_) * 32 + h_:
                                     (2 * c_ + s_) * 32 + h_ + 1]))(c, h, sbk, bA),
                        reads=[pbr[bA], r_dt],
                        writes=[rltu[h % 2]] if sbk == 1 else (),
                        deps=())
                P.op("scalar", (lambda h_: lambda en: en.activation(
                    ltb[h_ % 2][:, :], ltu[h_ % 2][:, :], AF.Exp, scale=-1.0))(h),
                    reads=[rltu[h % 2]], writes=[rltb[h % 2]])
                P.op("vector", (lambda h_: lambda en: en.tensor_tensor(
                    mt_sb[h_ % 2][:, :], ltb[h_ % 2][:, :], gtm_sb[:, :],
                    ALU.mult))(h),
                    reads=[rltb[h % 2], rgtm], writes=[rmt[h % 2]])
                mmt = None
                for lh in range(2):
                    for sbk in range(2):
                        mmt = P.op("tensor", (lambda h_, lh_, s_, b_: lambda en:
                             en.matmul(
                                 pb[b_][:, lh_ * 128:(lh_ + 1) * 128],
                                 mt_sb[h_ % 2][:, s_ * 256 + lh_ * 128:
                                               s_ * 256 + (lh_ + 1) * 128],
                                 xdt_a[s_][:, h_ * 128:(h_ + 1) * 128],
                                 start=(s_ == 0), stop=(s_ == 1)))(h, lh, sbk, bB),
                            reads=[rmt[h % 2], rxdt[sbk]],
                            writes=[pbr[bB]] if (lh == 0 and sbk == 0) else (),
                            deps=() if mmt is None else [mmt])
                for sbk in range(2):
                    P.op("vector", (lambda h_, s_: lambda en: en.tensor_tensor(
                        xdd[h_ % 2][:, s_ * 128:(s_ + 1) * 128],
                        xdt_a[s_][:, h_ * 128:(h_ + 1) * 128],
                        ds_tm[:, s_ * 32 + h_:s_ * 32 + h_ + 1]
                        .broadcast_to([128, 128]), ALU.mult))(h, sbk),
                        reads=[rxdt[sbk], rdstm],
                        writes=[rxdd[h % 2]] if sbk == 1 else (),
                        deps=())
                sct = None
                for sbk in range(2):
                    sct = P.op("tensor", (lambda h_, s_, b_: lambda en: en.matmul(
                        pb[b_][:, 256:384], btm[:, s_ * 128:(s_ + 1) * 128],
                        xdd[h_ % 2][:, s_ * 128:(s_ + 1) * 128],
                        start=(s_ == 0), stop=(s_ == 1)))(h, sbk, bB),
                        reads=[rbtm, rxdd[h % 2]],
                        deps=() if sct is None else [sct])
                pbr[bB].w = mmt if sct is None else sct
                if c > 0:
                    yot = None
                    for lh in range(2):
                        yot = P.op("tensor", (lambda h_, lh_, b_: lambda en:
                             en.matmul(
                                 pb[b_][:, lh_ * 128:(lh_ + 1) * 128],
                                 hbc_ch[:, 33 * 259 + lh_ * 128:
                                        33 * 259 + (lh_ + 1) * 128],
                                 st_bf[:, h_ * 128:(h_ + 1) * 128],
                                 start=True, stop=True))(h, lh, bC),
                            reads=[rhbc, rstbf[h]],
                            writes=[pbr[bC]] if lh == 0 else (),
                            deps=() if yot is None else [yot])
                    pbr[bC].w = yot
                    for lh in range(2):
                        P.op("scalar", (lambda h_, lh_, b_: lambda en:
                             en.activation(
                                 t1_sb[h_ % 2][:, lh_ * 128:(lh_ + 1) * 128],
                                 pb[b_][:, lh_ * 128:(lh_ + 1) * 128], AF.Identity,
                                 scale=sdx_tm[:, lh_ * 32 + h_:
                                              lh_ * 32 + h_ + 1]))(h, lh, bC),
                            reads=[pbr[bC], rsdx],
                            writes=[rt1[h % 2]] if lh == 1 else (),
                            deps=())
                for hf in range(2):
                    P.op("vector", (lambda h_, f_: lambda en: en.tensor_tensor(
                        xsd[h_ % 2][:, f_ * 128:(f_ + 1) * 128],
                        xs_tm[f_][:, h_ * 128:(h_ + 1) * 128],
                        dcol[:, h_:h_ + 1].broadcast_to([128, 128]),
                        ALU.mult))(h, hf),
                        reads=[rxs[hf], rdcol],
                        writes=[rxsd[h % 2]] if hf == 1 else (),
                        deps=())
                if c > 0:
                    P.op("vector", (lambda h_: lambda en: en.tensor_tensor(
                        tsum[h_ % 2][:, :], t1_sb[h_ % 2][:, :],
                        xsd[h_ % 2][:, :], ALU.add))(h),
                        reads=[rt1[h % 2], rxsd[h % 2]], writes=[rtsum[h % 2]])
                    src_ts = rtsum
                else:
                    src_ts = rxsd
                P.op("vector", (lambda h_, b_, st_: lambda en: en.tensor_tensor(
                    ybf[h_ % 2][:, :], pb[b_][:, 0:256],
                    (tsum if st_ else xsd)[h_ % 2][:, :], ALU.add))(
                        h, bB, c > 0),
                    reads=[pbr[bB], src_ts[h % 2]], writes=[rybf[h % 2]])
                for lh in range(2):
                    bank = 6 + lh
                    P.op("tensor", (lambda h_, lh_, b_: lambda en: en.transpose(
                        pb[b_].ap().bitcast(BF16)[:, 0:128],
                        ybf[h_ % 2][:, lh_ * 128:(lh_ + 1) * 128],
                        identb[:, :]))(h, lh, bank),
                        reads=[rybf[h % 2]], writes=[pbr[bank]])
                    P.op("vector", (lambda h_, lh_, b_, sgb_: lambda en:
                         en.tensor_tensor(
                             y_fm[:, h_ * 256 + lh_ * 128:
                                  h_ * 256 + (lh_ + 1) * 128],
                             pb[b_].ap().bitcast(BF16)[:, 0:128],
                             sgb_[:, h_ * 256 + lh_ * 128:
                                  h_ * 256 + (lh_ + 1) * 128], ALU.mult))(
                                 h, lh, bank, sgb),
                        reads=[pbr[bank], rsgb],
                        writes=[ryfm[h]] if lh == 1 else (),
                        deps=())
                if c > 0:
                    P.op("gpsimd", (lambda h_: lambda en: en.tensor_tensor(
                        sdec[:, :], st_f[:, h_ * 128:(h_ + 1) * 128],
                        dbc[:, h_:h_ + 1].broadcast_to([128, 128]),
                        ALU.mult))(h),
                        reads=[rst[h], rdbc], writes=[rsdec])
                    P.op("vector", (lambda h_, b_: lambda en: en.tensor_tensor(
                        st_f[:, h_ * 128:(h_ + 1) * 128], sdec[:, :],
                        pb[b_][:, 256:384], ALU.add))(h, bB),
                        reads=[rsdec, pbr[bB]], writes=[rst[h]])
                else:
                    P.op("vector", (lambda h_, b_: lambda en: en.tensor_copy(
                        st_f[:, h_ * 128:(h_ + 1) * 128], pb[b_][:, 256:384]))(
                            h, bB),
                        reads=[pbr[bB]], writes=[rst[h]])
                P.op("gpsimd", (lambda h_: lambda en: en.tensor_copy(
                    st_bf[:, h_ * 128:(h_ + 1) * 128],
                    st_f[:, h_ * 128:(h_ + 1) * 128]))(h),
                    reads=[rst[h]], writes=[rstbf[h]])
                if h % 2 == 1 and h // 2 + 2 < 16:
                    row_dma(h // 2 + 2)
            vmt = None
            for sl in range(32):
                P.op("scalar", (lambda sl_: lambda en: en.activation(
                    yg2[sl_ % 2][:, :], y_fm[:, sl_ * 256:(sl_ + 1) * 256],
                    AF.Square))(sl),
                    reads=[ryfm[sl]], writes=[ryg2[sl % 2]])
                for tb2 in range(2):
                    vmt = P.op("tensor", (lambda sl_, t_: lambda en: en.matmul(
                        pb[6][:, 400 + t_:401 + t_],
                        yg2[sl_ % 2][:, t_ * 128:(t_ + 1) * 128],
                        onescol_bf[:, :],
                        start=(sl_ == 0), stop=(sl_ == 31)))(sl, tb2),
                        reads=[ryg2[sl % 2]],
                        writes=[pbr[6]] if (sl == 0 and tb2 == 0) else (),
                        deps=() if vmt is None else [vmt])
            pbr[6].w = vmt
            P.op("scalar", lambda en: en.activation(
                var_sb[:, 0:2], pb[6][:, 400:402], AF.Ln, scale=1.0 / 4096,
                bias=EPS), reads=[pbr[6]], writes=[rvar])
            P.op("scalar", lambda en: en.activation(
                rstd_tm[:, 0:2], var_sb[:, 0:2], AF.Exp, scale=-0.5),
                reads=[rvar], writes=[rrstd])
            for jt in range(8):
                wob = wo_t[jt % 2]; rwob = rwot[jt % 2]
                P.op("sync", (lambda jt_, wob_: lambda en: en.dma_start(
                    wob_[:, :].rearrange("p (a b) -> p a b", a=32),
                    woutb_d.ap()[:, :, jt_ * 256:(jt_ + 1) * 256]))(jt, wob),
                    reads=[rwoutb], writes=[rwob], dma=True)
                for tb2 in range(2):
                    bank = 1 + 3 * (jt % 2)
                    mmo = None
                    for isl in range(32):
                        mmo = P.op("tensor", (lambda jt_, t_, i_, b_, wob_:
                             lambda en: en.matmul(
                                 pb[b_][:, 0:256],
                                 y_fm[:, i_ * 256 + t_ * 128:
                                      i_ * 256 + (t_ + 1) * 128],
                                 wob_[:, i_ * 256:(i_ + 1) * 256],
                                 start=(i_ == 0), stop=(i_ == 31)))(
                                     jt, tb2, isl, bank, wob),
                            reads=[ryfm[isl], rwob],
                            writes=[pbr[bank]] if isl == 0 else (),
                            deps=() if mmo is None else [mmo])
                    pbr[bank].w = mmo
                    P.op("vector", (lambda jt_, t_, b_: lambda en:
                         en.tensor_tensor(
                             resst_t[:, 0:256], pb[b_][:, 0:256],
                             rstd_tm[:, t_:t_ + 1].broadcast_to([128, 256]),
                             ALU.mult))(jt, tb2, bank),
                        reads=[pbr[bank], rrstd], writes=[rresq])
                    P.op("vector", (lambda jt_, t_: lambda en:
                         en.tensor_tensor(
                             xres[t_][:, jt_ * 256:(jt_ + 1) * 256],
                             xres[t_][:, jt_ * 256:(jt_ + 1) * 256],
                             resst_t[:, 0:256], ALU.add))(jt, tb2),
                        reads=[rresq],
                        writes=[rxres[tb2]] if jt == 7 else (),
                        deps=())
            for tb2 in range(2):
                P.op("sync", (lambda c_, t_: lambda en: en.dma_start(
                    res_d.ap()[:, c_ * 2 + t_, :], xres[t_][:, :]))(c, tb2),
                    reads=[rxres[tb2]], writes=[rresd[c * 2 + tb2]], dma=True)
        p2.close()
        p12.close()
        P.fence()

        p3 = ExitStack()
        xnh = sbt(p3, "xnh", [128, KS * HALF], BF16)
        rxnh = R("xnh")
        acc = sbt(p3, "acc", [128, 8 * H])
        racc = [R(f"acc{tb}") for tb in range(8)]
        gwb = sbt(p3, "gwb", [128, KS * NI], BF16)
        rgwb = R("gwb")
        uwb = sbt(p3, "uwb", [128, KS * NI], BF16)
        ruwb = R("uwb")
        dwb = sbt(p3, "dwb", [128, NIB * H], BF16)
        rdwb = R("dwb")
        hfm = sbt(p3, "hfm", [128, NIB * HALF], BF16)
        rhfm = [R(f"hfm{i}") for i in range(NIB)]
        sgt = sbt(p3, "sgt", [128, HALF], BF16)
        rsgt = R("sgt")
        upw = sbt(p3, "upw", [128, HALF], BF16)
        rupw = R("upw")
        wrow = sbt(p3, "wrow", [1, HALF])
        rwrow = R("wrow")
        wbc = sbt(p3, "wbc", [128, HALF])
        rwbc = R("wbc")
        rtile = [sbt(p3, "rtile0", [128, H])]
        rrt = [R("rt0")]

        for HF in range(2):
            P.op("sync", (lambda HF_: lambda en: en.dma_start(
                xnh[:, :].rearrange("p (a b) -> p a b", a=KS),
                xn_d.ap()[:, :, HF_ * HALF:(HF_ + 1) * HALF]))(HF),
                reads=[rxnd], writes=[rxnh], dma=True)
            for e in range(NE):
                first = (e == 0)
                if HF == 0:
                    P.op("gpsimd", (lambda e_: lambda en: en.dma_start(
                        gwb[:, :].rearrange("p (a b) -> p a b", a=KS),
                        gw_d.ap()[e_].transpose([1, 0, 2])))(e),
                        writes=[rgwb], dma=True)
                    P.op("gpsimd", (lambda e_: lambda en: en.dma_start(
                        uwb[:, :].rearrange("p (a b) -> p a b", a=KS),
                        uw_d.ap()[e_].transpose([1, 0, 2])))(e),
                        writes=[ruwb], dma=True)
                    P.op("gpsimd", (lambda e_: lambda en: en.dma_start(
                        dwb[:, :].rearrange("p (a b) -> p a b", a=NIB),
                        dw_d.ap()[e_].transpose([1, 0, 2])))(e),
                        writes=[rdwb], dma=True)
                    P.op("sync", (lambda e_: lambda en: en.dma_start(
                        moeg_d.ap()[e_], gwb[:, :]))(e),
                        reads=[rgwb], writes=[rmoeg[e]], dma=True)
                    P.op("sync", (lambda e_: lambda en: en.dma_start(
                        moeu_d.ap()[e_], uwb[:, :]))(e),
                        reads=[ruwb], writes=[rmoeu[e]], dma=True)
                    P.op("sync", (lambda e_: lambda en: en.dma_start(
                        moed_d.ap()[e_], dwb[:, :]))(e),
                        reads=[rdwb], writes=[rmoed[e]], dma=True)
                else:
                    P.op("gpsimd", (lambda e_: lambda en: en.dma_start(
                        gwb[:, :], moeg_d.ap()[e_]))(e),
                        reads=[rmoeg[e]], writes=[rgwb], dma=True)
                    P.op("gpsimd", (lambda e_: lambda en: en.dma_start(
                        uwb[:, :], moeu_d.ap()[e_]))(e),
                        reads=[rmoeu[e]], writes=[ruwb], dma=True)
                    P.op("gpsimd", (lambda e_: lambda en: en.dma_start(
                        dwb[:, :], moed_d.ap()[e_]))(e),
                        reads=[rmoed[e]], writes=[rdwb], dma=True)
                for tb in range(8):
                    gtb = HF * 8 + tb
                    bank = 6 + tb % 2
                    P.op("tensor", (lambda g_, e_, b_: lambda en: en.transpose(
                        pb[b_][0:1, 0:128],
                        w_all[:, g_ * 32 + e_:g_ * 32 + e_ + 1], identf[:, :]))(
                            gtb, e, bank),
                        reads=[r_wall], writes=[pbr[bank]])
                    P.op("vector", (lambda tb_, b_: lambda en: en.tensor_copy(
                        wrow[:, tb_ * 128:(tb_ + 1) * 128], pb[b_][0:1, 0:128]))(
                            tb, bank),
                        reads=[pbr[bank]],
                        writes=[rwrow] if tb == 7 else (),
                        deps=())
                for hw in range(2):
                    bank = 6 + hw
                    P.op("tensor", (lambda hw_, b_: lambda en: en.matmul(
                        pb[b_][:, :], ones1_sb[:, :],
                        wrow[:, hw_ * 512:(hw_ + 1) * 512], start=True,
                        stop=True))(hw, bank),
                        reads=[rwrow], writes=[pbr[bank]])
                    P.op("vector", (lambda hw_, b_: lambda en: en.tensor_copy(
                        wbc[:, hw_ * 512:(hw_ + 1) * 512], pb[b_][:, :]))(hw, bank),
                        reads=[pbr[bank]],
                        writes=[rwbc] if hw == 1 else (),
                        deps=())
                for ib in range(NIB):
                    for tt in range(2):
                        gmt = None
                        for ksl in range(KS):
                            gmt = P.op("tensor", (lambda i_, t_, k_: lambda en:
                                 en.matmul(
                                     pb[t_][:, 0:512],
                                     gwb[:, k_ * NI + i_ * 128:
                                         k_ * NI + (i_ + 1) * 128],
                                     xnh[:, k_ * HALF + t_ * 512:
                                         k_ * HALF + (t_ + 1) * 512],
                                     start=(k_ == 0), stop=(k_ == KS - 1)))(
                                         ib, tt, ksl),
                                reads=[rgwb, rxnh],
                                writes=[pbr[tt]] if ksl == 0 else (),
                                deps=() if gmt is None else [gmt])
                        pbr[tt].w = gmt
                    for tt in range(2):
                        umt = None
                        for ksl in range(KS):
                            umt = P.op("tensor", (lambda i_, t_, k_: lambda en:
                                 en.matmul(
                                     pb[2 + t_][:, 0:512],
                                     uwb[:, k_ * NI + i_ * 128:
                                         k_ * NI + (i_ + 1) * 128],
                                     xnh[:, k_ * HALF + t_ * 512:
                                         k_ * HALF + (t_ + 1) * 512],
                                     start=(k_ == 0), stop=(k_ == KS - 1)))(
                                         ib, tt, ksl),
                                reads=[ruwb, rxnh],
                                writes=[pbr[2 + tt]] if ksl == 0 else (),
                                deps=() if umt is None else [umt])
                        pbr[2 + tt].w = umt
                    for tt in range(2):
                        P.op("scalar", (lambda t_: lambda en: en.activation(
                            sgt[:, t_ * 512:(t_ + 1) * 512], pb[t_][:, 0:512],
                            AF.Silu))(tt),
                            reads=[pbr[tt]],
                            writes=[rsgt] if tt == 1 else (),
                            deps=())
                        P.op("vector", (lambda t_: lambda en: en.tensor_tensor(
                            upw[:, t_ * 512:(t_ + 1) * 512], pb[2 + t_][:, 0:512],
                            wbc[:, t_ * 512:(t_ + 1) * 512], ALU.mult))(tt),
                            reads=[pbr[2 + tt], rwbc],
                            writes=[rupw] if tt == 1 else (),
                            deps=())
                    P.op("vector", (lambda i_: lambda en: en.tensor_tensor(
                        hfm[:, i_ * HALF:(i_ + 1) * HALF], sgt[:, :], upw[:, :],
                        ALU.mult))(ib),
                        reads=[rsgt, rupw], writes=[rhfm[ib]])
                for tb in range(8):
                    for jt in range(4):
                        bank = 4 + (tb * 4 + jt) % 2
                        dnt = None
                        for isl in range(NIB):
                            dnt = P.op("tensor", (lambda tb_, jt_, i_, b_:
                                 lambda en: en.matmul(
                                     pb[b_][:, 0:512],
                                     hfm[:, i_ * HALF + tb_ * 128:
                                         i_ * HALF + (tb_ + 1) * 128],
                                     dwb[:, i_ * H + jt_ * 512:
                                         i_ * H + (jt_ + 1) * 512],
                                     start=(i_ == 0), stop=(i_ == NIB - 1)))(
                                         tb, jt, isl, bank),
                                reads=[rhfm[isl], rdwb],
                                writes=[pbr[bank]] if isl == 0 else (),
                                deps=() if dnt is None else [dnt])
                        pbr[bank].w = dnt
                        if first:
                            P.op("vector", (lambda tb_, jt_, b_: lambda en:
                                 en.tensor_copy(
                                     acc[:, tb_ * H + jt_ * 512:
                                         tb_ * H + (jt_ + 1) * 512],
                                     pb[b_][:, 0:512]))(tb, jt, bank),
                                reads=[pbr[bank]],
                                writes=[racc[tb]] if jt == 3 else (),
                                deps=())
                        else:
                            P.op("vector", (lambda tb_, jt_, b_: lambda en:
                                 en.tensor_tensor(
                                     acc[:, tb_ * H + jt_ * 512:
                                         tb_ * H + (jt_ + 1) * 512],
                                     acc[:, tb_ * H + jt_ * 512:
                                         tb_ * H + (jt_ + 1) * 512],
                                     pb[b_][:, 0:512], ALU.add))(tb, jt, bank),
                                reads=[pbr[bank]],
                                writes=[racc[tb]] if jt == 3 else (),
                                deps=())
            for tb in range(8):
                gtb = HF * 8 + tb
                rt = rtile[0]; rr = rrt[0]
                P.op("sync", (lambda g_, rt_: lambda en: en.dma_start(
                    rt_[:, :], res_d.ap()[:, g_, :]))(gtb, rt),
                    reads=[rresd[gtb]], writes=[rr], dma=True)
                P.op("vector", (lambda tb_, rt_: lambda en: en.tensor_tensor(
                    rt_[:, :], rt_[:, :], acc[:, tb_ * H:(tb_ + 1) * H],
                    ALU.add))(tb, rt),
                    reads=[rr, racc[tb]], writes=[rr])
                P.op("sync", (lambda g_, rt_: lambda en: en.dma_start(
                    out_d.ap()[g_ * 128:(g_ + 1) * 128, :], rt_[:, :]))(gtb, rt),
                    reads=[rr], dma=True)
        p3.close()

        if debug:
            dbgt = sbt(stack, "dbgt", [128, 4096])
            P.op("vector", lambda en: en.tensor_copy(dbgt[:, 0:512], w_all[:, :]),
                 reads=[r_wall])
            P.op("vector", lambda en: en.tensor_copy(dbgt[:, 3584:3680],
                                                     ldbg[:, :]), reads=[rldbg])
            P.op("vector", lambda en: en.tensor_copy(dbgt[:, 512:1024],
                                                     dt_tm[:, :]), reads=[r_dt])
            P.op("vector", lambda en: en.tensor_copy(dbgt[:, 1024:1536],
                                                     acum_tm[:, :]), reads=[r_dt])
            P.op("vector", lambda en: en.tensor_copy(
                dbgt[0:32, 1536:3584], dt_fm[:, :]), reads=[r_dt])
            P.op("sync", lambda en: en.dma_start(dbgv_d.ap(), dbgt[:, :]), dma=True)

        blk = stack.enter_context(nc.Block("main"))
        P.emit(nc, blk, stack)
    return nc


def _prep(inputs):
    wip = inputs["L0_attn_in_proj_w"]
    wip_p = np.zeros((JPAD, H), np.float32)
    wip_p[:8480] = wip
    wip_b = np.ascontiguousarray(
        wip_p.reshape(NJB, 128, KS, 128).transpose(0, 2, 3, 1))
    wout_b = np.ascontiguousarray(
        inputs["L0_attn_out_proj_w"].T.reshape(32, 128, H))
    gw_b = np.ascontiguousarray(
        inputs["L0_ffn_gate_w"].transpose(0, 2, 1).reshape(NE, KS, 128, NI))
    uw_b = np.ascontiguousarray(
        inputs["L0_ffn_up_w"].transpose(0, 2, 1).reshape(NE, KS, 128, NI))
    dw_b = np.ascontiguousarray(
        inputs["L0_ffn_down_w"].transpose(0, 2, 1).reshape(NE, NIB, 128, H))
    rwt_b = np.ascontiguousarray(
        inputs["L0_ffn_router_w"].T.reshape(KS, 128, NE).transpose(1, 0, 2))
    convw = np.ascontiguousarray(
        inputs["L0_attn_conv_w"][:, 0, :].reshape(34, 128, 4).transpose(1, 0, 2))
    convb = np.ascontiguousarray(inputs["L0_attn_conv_b"].reshape(34, 128).T)
    nw = np.ascontiguousarray(inputs["L0_attn_norm_w"].reshape(32, 128).T)
    vecs = np.ascontiguousarray(np.stack([
        inputs["L0_attn_A_log"], inputs["L0_attn_dt_bias"], inputs["L0_attn_D"]],
        axis=1))
    consts = np.tile(np.asarray(CONST_VALS, np.float32)[None, :], (128, 1))
    tri = np.zeros((128, 512), np.float32)
    for sp in range(128):
        tri[sp, sp:256] = 1.0
        if 128 + sp < 256:
            tri[sp, 256 + 128 + sp:512] = 1.0
    ident = np.eye(128, dtype=np.float32)
    ones1 = np.ones((1, 128), np.float32)
    onescol = np.ones((128, 1), np.float32)
    return {
        "wip": wip_b, "wout": wout_b, "gw": gw_b, "uw": uw_b, "dw": dw_b,
        "rwt": rwt_b, "convw": convw, "convb": convb, "nw": nw, "vecs": vecs,
        "consts": consts, "tri": tri, "ident": ident, "ones1": ones1,
        "onescol": onescol,
    }


def kernel(**inputs):
    shared = _prep(inputs)
    x = inputs["L0_attn_hidden_states"]
    nc = build_nc(debug=False)
    in_maps = []
    for c in range(NCORES):
        m = dict(shared)
        m["x"] = np.ascontiguousarray(x[c])
        in_maps.append(m)
    res = run_bass_kernel_spmd(nc, in_maps, core_ids=list(range(NCORES)))
    return np.stack([np.asarray(r["out"]) for r in res.results], axis=0)


# version 07
# speedup vs baseline: 1.5997x; 1.5997x over previous
import numpy as np
from contextlib import ExitStack

import concourse.bass as bass
import concourse.mybir as mybir
from concourse.bass_utils import run_bass_kernel_spmd
from concourse.bass import IndirectOffsetOnAxis

F32 = mybir.dt.float32
BF16 = mybir.dt.bfloat16
I32 = mybir.dt.int32
AF = mybir.ActivationFunctionType
ALU = mybir.AluOpType
AX = mybir.AxisListType

NCORES = 8
S = 2048
H = 2048
KS = 16
INTER = 4096
JPAD = 8576
NJB = 67
HEADS = 32
CHUNK = 256
NCH = 8
NTB = 16
NE = 32
NI = 768
NIB = 6
HALF = 1024
CAP = 384
EPS = 1e-5

CONST_VALS = [0.0, 1.0, -1.0, -2.0, 2.0, 0.5, -0.5, 100.0, EPS,
              1.0 / 2048, -1.0 / 2048, 1.0 / 4096, 12288.0]

ENGS = ("sync", "tensor", "vector", "scalar", "gpsimd")


class Rsrc:
    __slots__ = ("w", "rs", "name")

    def __init__(self, name=""):
        self.w = None
        self.rs = []
        self.name = name


class Prog:

    def __init__(self):
        self.ops = {e: [] for e in ENGS}
        self.marked = {e: [] for e in ENGS}
        self.pending = {}

    def op(self, eng, fn, reads=(), writes=(), deps=(), dma=False,
           sync_after=False):
        dl = list(deps)
        dl.extend(self.pending.pop(eng, ()))
        for r in reads:
            if r.w is not None:
                dl.append(r.w)
        for r in writes:
            if r.w is not None:
                dl.append(r.w)
            dl.extend(r.rs)
        idx = len(self.ops[eng])
        tok = (eng, idx)
        seen = set()
        fl = []
        for d in dl:
            if d is None or d in seen:
                continue
            seen.add(d)
            if d[0] == eng:
                continue
            fl.append(d)
        self.ops[eng].append((fn, fl, dma, sync_after))
        self.marked[eng].append(dma or sync_after)
        for d in fl:
            self.marked[d[0]][d[1]] = True
        for r in reads:
            r.rs.append(tok)
        for r in writes:
            r.w = tok
            r.rs = []
        return tok

    def barrier(self):
        toks = []
        for e in ENGS:
            n = len(self.ops[e])
            if n:
                toks.append((e, n - 1))
        return toks

    def fence(self):
        toks = self.barrier()
        for e in ENGS:
            self.pending.setdefault(e, []).extend(toks)

    def emit(self, nc, block, stack):
        sems = {e: stack.enter_context(nc.semaphore(f"esem_{e}")) for e in ENGS}
        vals = {}
        for e in ENGS:
            acc = 0
            va = []
            for i, m in enumerate(self.marked[e]):
                if m:
                    acc += 16 if self.ops[e][i][2] else 1
                va.append(acc)
            vals[e] = va

        def section(engname):
            def body(eng):
                memo = {}
                for i, (fn, deps, is_dma, sync_after) in enumerate(
                        self.ops[engname]):
                    need = {}
                    for (de, di) in deps:
                        v = vals[de][di]
                        if v > need.get(de, 0):
                            need[de] = v
                    for de, v in need.items():
                        if memo.get(de, 0) >= v:
                            continue
                        eng.wait_ge(sems[de], v)
                        memo[de] = v
                    inst = fn(eng)
                    if self.marked[engname][i]:
                        inst.then_inc(sems[engname], 16 if is_dma else 1)
                    if sync_after:
                        eng.wait_ge(sems[engname], vals[engname][i])
                if engname == "sync" and vals["sync"]:
                    eng.wait_ge(sems["sync"], vals["sync"][-1])
            return body

        block.sync(section("sync"))
        block.tensor(section("tensor"))
        block.vector(section("vector"))
        block.scalar(section("scalar"))
        block.gpsimd(section("gpsimd"))


def build_nc(debug=False):
    nc = bass.Bass("TRN2", target_bir_lowering=False)

    x_d = nc.dram_tensor("x", [S, H], F32, kind="ExternalInput")
    wip_d = nc.dram_tensor("wip", [NJB, KS, 128, 128], F32, kind="ExternalInput")
    wout_d = nc.dram_tensor("wout", [32, 128, H], F32, kind="ExternalInput")
    gw_d = nc.dram_tensor("gw", [NE, KS, 128, NI], F32, kind="ExternalInput")
    uw_d = nc.dram_tensor("uw", [NE, KS, 128, NI], F32, kind="ExternalInput")
    dw_d = nc.dram_tensor("dw", [NE, NIB, 128, H], F32, kind="ExternalInput")
    rwt_d = nc.dram_tensor("rwt", [128, KS, NE], F32, kind="ExternalInput")
    convw_d = nc.dram_tensor("convw", [128, 34, 4], F32, kind="ExternalInput")
    convb_d = nc.dram_tensor("convb", [128, 34], F32, kind="ExternalInput")
    nw_d = nc.dram_tensor("nw", [128, 32], F32, kind="ExternalInput")
    vecs_d = nc.dram_tensor("vecs", [32, 3], F32, kind="ExternalInput")
    consts_d = nc.dram_tensor("consts", [128, len(CONST_VALS)], F32,
                              kind="ExternalInput")
    tri_d = nc.dram_tensor("tri", [128, 512], F32, kind="ExternalInput")
    ident_d = nc.dram_tensor("ident", [128, 128], F32, kind="ExternalInput")
    ones1_d = nc.dram_tensor("ones1", [1, 128], F32, kind="ExternalInput")
    onescol_d = nc.dram_tensor("onescol", [128, 1], F32, kind="ExternalInput")
    tokid_d = nc.dram_tensor("tokid", [128, NTB], I32, kind="ExternalInput")
    erow_d = nc.dram_tensor("erow", [128, NE], F32, kind="ExternalInput")
    ecr_d = nc.dram_tensor("ecr", [128, NE], F32, kind="ExternalInput")

    out_d = nc.dram_tensor("out", [S, H], F32, kind="ExternalOutput")
    SCR = "ExternalOutput" if debug else "Internal"
    xn_d = nc.dram_tensor("xn_scr", [128, KS, S], BF16, kind=SCR)
    gate_d = nc.dram_tensor("gate_scr", [128, 32, S], BF16, kind=SCR)
    hbc_d = nc.dram_tensor("hbc_scr", [128, 34, S + 3], BF16, kind=SCR)
    res_d = nc.dram_tensor("res_scr", [128, NTB, H], F32, kind=SCR)
    woutb_d = nc.dram_tensor("woutb_scr", [128, 32, H], BF16, kind="Internal")
    acrow_d = nc.dram_tensor("acrow_scr", [NCH, 32, 256], F32, kind="Internal")
    xntm_d = nc.dram_tensor("xntm_scr", [S, H], BF16, kind=SCR)
    idxg_d = nc.dram_tensor("idxg_scr", [NE * CAP + 128, 1], I32, kind=SCR)
    ybuf_d = nc.dram_tensor("ybuf_scr", [NE * CAP + 1, H], F32, kind=SCR)
    dbgv_d = nc.dram_tensor("dbgv_scr", [128, 4096], F32,
                            kind="ExternalOutput") if debug else None

    P = Prog()
    R = Rsrc

    stack = ExitStack()
    with stack:
        def sbt(st, name, shape, dtype=F32):
            return st.enter_context(nc.sbuf_tensor(name, shape, dtype))

        pb = [nc.alloc_psum_tensor(f"pb{i}", [128, 512], F32) for i in range(8)]
        pbr = [R(f"pb{i}") for i in range(8)]

        consts_sb = sbt(stack, "consts_sb", [128, len(CONST_VALS)])
        for ci, cv in enumerate(CONST_VALS):
            nc.const_aps.register(F32, cv, consts_sb[:, ci:ci + 1])
        identf = sbt(stack, "identf", [128, 128])
        identb = sbt(stack, "identb", [128, 128], BF16)
        ones1_sb = sbt(stack, "ones1_sb", [1, 128])
        onescol_bf = sbt(stack, "onescol_bf", [128, 1], BF16)
        vecs_sb = sbt(stack, "vecs_sb", [32, 3])
        acol_sb = sbt(stack, "acol_sb", [32, 1])
        w_all = sbt(stack, "w_all", [128, NTB * 32])
        eid_sb = sbt(stack, "eid_sb", [128, NTB * 4])
        gidx_sb = sbt(stack, "gidx_sb", [128, NTB * 4], I32)
        wr_sb = sbt(stack, "wr_sb", [128, NTB * 4])
        tokid_sb = sbt(stack, "tokid_sb", [128, NTB], I32)
        erow_sb = sbt(stack, "erow_sb", [128, NE])
        ecr_sb = sbt(stack, "ecr_sb", [128, NE])
        reid = R("eid"); rgidx = R("gidx"); rwr = R("wr")
        rtokid = R("tokid"); rerow = R("erow"); recr = R("ecr")
        rxntm_d = R("xntm_d"); ridxg = R("idxg"); rybuf = R("ybuf")
        ldbg = sbt(stack, "ldbg", [128, 96], BF16) if debug else None
        rldbg = R("ldbg")
        p12 = ExitStack()
        tri_sb = sbt(p12, "tri_sb", [128, 512])
        convw_sb = sbt(p12, "convw_sb", [128, 34 * 4])
        convb_sb = sbt(p12, "convb_sb", [128, 34])
        nw_sb = sbt(p12, "nw_sb", [128, 32])
        acum_fm = sbt(p12, "acum_fm", [32, S])
        dt_tm = sbt(p12, "dt_tm", [128, NTB * 32])
        acum_tm = sbt(p12, "acum_tm", [128, NTB * 32])

        zpad = sbt(p12, "zpad", [128, 102], BF16)
        zidx = sbt(p12, "zidx", [128, 97], I32)
        zrow = sbt(p12, "zrow", [128, 16])
        rzidx = R("zidx"); rzrow = R("zrow")
        rwt_bf = sbt(p12, "rwt_bf", [128, KS * 32], BF16)

        r_wall = R("wall")
        r_dt = R("dtgrp")
        r_acol = R("acol")
        r_zpad = R("zpad")
        rmoeg = [R(f"moeg{e}") for e in range(NE)]
        rmoeu = [R(f"moeu{e}") for e in range(NE)]
        rmoed = [R(f"moed{e}") for e in range(NE)]
        rwoutb = R("woutb")
        rxnd = R("xn_dram")
        rgated = R("gate_dram")
        rhbcd = R("hbc_dram")
        rresd = [R(f"res_d{i}") for i in range(NTB)]

        for d, s in [
            (consts_sb[:, :], consts_d.ap()),
            (tri_sb[:, :], tri_d.ap()),
            (identf[:, :], ident_d.ap()),
            (ones1_sb[:, :], ones1_d.ap()),
            (convw_sb[:, :], convw_d.ap().rearrange("p a b -> p (a b)")),
            (convb_sb[:, :], convb_d.ap()),
            (nw_sb[:, :], nw_d.ap()),
            (vecs_sb[:, :], vecs_d.ap()),
        ]:
            P.op("sync", (lambda d_, s_: lambda en: en.dma_start(d_, s_))(d, s),
                 dma=True)
        P.op("gpsimd", lambda en: en.dma_start(identb[:, :], ident_d.ap()), dma=True)
        P.op("sync", lambda en: en.dma_start(tokid_sb[:, :], tokid_d.ap()),
             writes=[rtokid], dma=True)
        P.op("sync", lambda en: en.dma_start(erow_sb[:, :], erow_d.ap()),
             writes=[rerow], dma=True)
        P.op("sync", lambda en: en.dma_start(ecr_sb[:, :], ecr_d.ap()),
             writes=[recr], dma=True)
        P.op("gpsimd", lambda en: en.dma_start(onescol_bf[:, :], onescol_d.ap()),
             dma=True)
        P.op("gpsimd", lambda en: en.dma_start(
            rwt_bf[:, :], rwt_d.ap().rearrange("p a b -> p (a b)")), dma=True)
        P.fence()
        tA = P.op("scalar", lambda en: en.activation(acol_sb[:, :], vecs_sb[:, 0:1],
                                                     AF.Exp), writes=[r_acol])
        P.op("vector", lambda en: en.tensor_scalar(acol_sb[:, :], acol_sb[:, :],
                                                   -1.0, None, ALU.mult),
             reads=[r_acol], writes=[r_acol])
        P.op("vector", lambda en: en.memset(ones32[:, :], 1.0))
        P.op("vector", lambda en: en.memset(zero32[:, :], 0.0))
        P.op("vector", lambda en: en.memset(zpad[:, :], 0.0), writes=[r_zpad])
        P.op("vector", lambda en: en.memset(zidx[:, :], 0), writes=[rzidx])
        P.op("vector", lambda en: en.memset(zrow[:, :], 0.0), writes=[rzrow])
        P.op("sync", lambda en: en.dma_start(
            idxg_d.ap().rearrange("(a b) c -> a (b c)", a=128), zidx[:, :]),
            reads=[rzidx], writes=[ridxg], dma=True)
        P.op("sync", lambda en: en.dma_start(
            ybuf_d.ap()[NE * CAP:NE * CAP + 1, :]
            .rearrange("o (a b) -> (o a) b", a=128), zrow[:, :]),
            reads=[rzrow], writes=[rybuf], dma=True)
        P.op("sync", lambda en: en.dma_start(
            hbc_d.ap()[:, :, 0:3], zpad[:, :].rearrange("p (a b) -> p a b", a=34)),
            reads=[r_zpad], writes=[rhbcd], dma=True)

        p01 = ExitStack()
        xn_fm = sbt(p01, "xn_fm", [128, KS * 2048], BF16)
        r_xnfm = R("xn_fm")
        ones32 = sbt(p01, "ones32", [32, S])
        zero32 = sbt(p01, "zero32", [32, 1])
        dt_fm = sbt(p01, "dt_fm", [32, S])

        p0 = ExitStack()
        xblk = [sbt(p0, "xblk0", [128, H]), sbt(p0, "xblk1", [128, H])]
        rxblk = [R("xblk0"), R("xblk1")]
        xsc = sbt(p0, "xsc", [128, H])
        sqjunk = sbt(p0, "sqjunk", [128, H])
        stat = sbt(p0, "stat", [128, 8])
        xn_tmb = sbt(p0, "xn_tmb", [128, H], BF16)
        prout = sbt(p0, "prout", [128, 128])
        routc = sbt(p0, "routc", [128, 4])
        eqe = sbt(p0, "eqe", [128, 32])
        rxsc = R("xsc"); rstat = R("stat"); rxntm = R("xn_tmb")
        rprout = R("prout"); rroutc = R("routc"); reqe = R("eqe")

        for tb in range(NTB):
            xb = xblk[tb % 2]
            rxb = rxblk[tb % 2]
            P.op("sync", (lambda xb_, tb_: lambda en: en.dma_start(
                xb_[:, :], x_d.ap()[tb_ * 128:(tb_ + 1) * 128, :]))(xb, tb),
                writes=[rxb], dma=True)
            P.op("vector", (lambda xb_: lambda en: en.reduce_sum(
                stat[:, 0:1], xb_[:, :], AX.X))(xb),
                reads=[rxb], writes=[rstat], sync_after=True)
            P.op("vector", lambda en: en.tensor_scalar(
                stat[:, 1:2], stat[:, 0:1], -1.0 / 2048, None, ALU.mult),
                reads=[rstat], writes=[rstat])
            P.op("scalar", (lambda xb_: lambda en: en.activation(
                xsc[:, :], xb_[:, :], AF.Identity, bias=stat[:, 1:2]))(xb),
                reads=[rxb, rstat], writes=[rxsc])
            P.op("scalar", lambda en: en.activation(
                sqjunk[:, :], xsc[:, :], AF.Square, accum_out=stat[:, 2:3]),
                reads=[rxsc], writes=[rstat])
            P.op("vector", lambda en: en.tensor_scalar(
                stat[:, 3:4], stat[:, 2:3], 1.0 / 2048, EPS, ALU.mult, ALU.add),
                reads=[rstat], writes=[rstat])
            P.op("scalar", lambda en: en.activation(
                stat[:, 4:5], stat[:, 3:4], AF.Ln),
                reads=[rstat], writes=[rstat])
            P.op("scalar", lambda en: en.activation(
                stat[:, 5:6], stat[:, 4:5], AF.Exp, scale=-0.5),
                reads=[rstat], writes=[rstat])
            P.op("vector", lambda en: en.tensor_tensor(
                stat[:, 6:7], stat[:, 1:2], stat[:, 5:6], ALU.mult),
                reads=[rstat], writes=[rstat])
            P.op("scalar", (lambda xb_: lambda en: en.activation(
                xn_tmb[:, :], xb_[:, :], AF.Identity, scale=stat[:, 5:6],
                bias=stat[:, 6:7]))(xb),
                reads=[rxb, rstat], writes=[rxntm])
            for ksl in range(KS):
                bank = (tb * KS + ksl) % 2 + 6
                P.op("tensor", (lambda k_, b_: lambda en: en.transpose(
                    pb[b_].ap().bitcast(BF16)[:, 0:128],
                    xn_tmb[:, k_ * 128:(k_ + 1) * 128], identb[:, :]))(ksl, bank),
                    reads=[rxntm], writes=[pbr[bank]])
                P.op("vector", (lambda k_, b_, tb_: lambda en: en.tensor_copy(
                    xn_fm[:, k_ * 2048 + tb_ * 128:k_ * 2048 + (tb_ + 1) * 128],
                    pb[b_].ap().bitcast(BF16)[:, 0:128]))(ksl, bank, tb),
                    reads=[pbr[bank]], writes=[r_xnfm])
            P.op("sync", (lambda tb_: lambda en: en.dma_start(
                xntm_d.ap()[tb_ * 128:(tb_ + 1) * 128, :], xn_tmb[:, :]))(tb),
                reads=[rxntm], writes=[rxntm_d] if tb == NTB - 1 else (),
                dma=True)
            lgt = None
            for ksl in range(KS):
                lgt = P.op("tensor", (lambda k_, tb_: lambda en: en.matmul(
                    pb[2][:, 0:32],
                    xn_fm[:, k_ * 2048 + tb_ * 128:k_ * 2048 + (tb_ + 1) * 128],
                    rwt_bf[:, k_ * 32:(k_ + 1) * 32],
                    start=(k_ == 0), stop=(k_ == KS - 1)))(ksl, tb),
                    reads=[r_xnfm],
                    writes=[pbr[2]] if ksl == 0 else (),
                    deps=() if lgt is None else [lgt])
            pbr[2].w = lgt
            if debug and tb == NTB - 1:
                P.op("vector", lambda en: en.tensor_copy(
                    ldbg[:, 0:32], pb[2][:, 0:32]), reads=[pbr[2]], writes=[rldbg])
            probs = prout[:, 0:32]; cur = prout[:, 32:64]
            selm = prout[:, 64:96]; psel = prout[:, 96:128]
            P.op("vector", lambda en: en.reduce_max(
                routc[:, 0:1], pb[2][:, 0:32], AX.X, negate=True),
                reads=[pbr[2]], writes=[rroutc], sync_after=True)
            P.op("scalar", lambda en: en.activation(
                prout[:, 0:32], pb[2][:, 0:32], AF.Exp, bias=routc[:, 0:1]),
                reads=[pbr[2], rroutc], writes=[rprout])
            if debug and tb == NTB - 1:
                P.op("vector", lambda en: en.tensor_copy(
                    ldbg[:, 32:64], prout[:, 0:32]), reads=[rprout], writes=[rldbg])
                P.op("vector", lambda en: en.tensor_copy(
                    ldbg[:, 64:65], routc[:, 0:1]), reads=[rroutc], writes=[rldbg])
            P.op("vector", lambda en: en.tensor_copy(prout[:, 32:64],
                                                     prout[:, 0:32]),
                 reads=[rprout], writes=[rprout])
            P.op("vector", lambda en: en.memset(prout[:, 64:96], 0.0),
                 writes=[rprout])
            for r4 in range(4):
                P.op("vector", lambda en: en.reduce_max(
                    routc[:, 3:4], prout[:, 32:64], AX.X),
                    reads=[rprout], writes=[rroutc], sync_after=True)
                P.op("vector", lambda en: en.tensor_tensor(
                    prout[:, 96:128], prout[:, 32:64],
                    routc[:, 3:4].broadcast_to([128, 32]), ALU.is_ge),
                    reads=[rprout, rroutc], writes=[rprout])
                P.op("vector", lambda en: en.tensor_tensor(
                    prout[:, 64:96], prout[:, 64:96], prout[:, 96:128], ALU.add),
                    reads=[rprout], writes=[rprout])
                P.op("vector", lambda en: en.tensor_tensor(
                    eqe[:, :], prout[:, 96:128], erow_sb[:, :], ALU.mult),
                    reads=[rprout, rerow], writes=[reqe])
                P.op("vector", (lambda tb_, r_: lambda en: en.reduce_sum(
                    eid_sb[:, tb_ * 4 + r_:tb_ * 4 + r_ + 1], eqe[:, :],
                    AX.X))(tb, r4),
                    reads=[reqe], writes=[reid])
                P.op("vector", lambda en: en.scalar_tensor_tensor(
                    prout[:, 32:64], prout[:, 96:128], -2.0, prout[:, 32:64],
                    ALU.mult, ALU.add),
                    reads=[rprout], writes=[rprout])
            P.op("vector", lambda en: en.tensor_scalar(
                prout[:, 64:96], prout[:, 64:96], 1.0, None, ALU.min),
                reads=[rprout], writes=[rprout])
            P.op("vector", lambda en: en.tensor_tensor(
                prout[:, 96:128], prout[:, 0:32], prout[:, 64:96], ALU.mult),
                reads=[rprout], writes=[rprout])
            P.op("vector", lambda en: en.reduce_sum(
                routc[:, 1:2], prout[:, 96:128], AX.X),
                reads=[rprout], writes=[rroutc])
            P.op("scalar", lambda en: en.activation(
                routc[:, 2:3], routc[:, 1:2], AF.Ln),
                reads=[rroutc], writes=[rroutc])
            P.op("scalar", lambda en: en.activation(
                routc[:, 2:3], routc[:, 2:3], AF.Exp, scale=-1.0),
                reads=[rroutc], writes=[rroutc])
            P.op("vector", (lambda tb_: lambda en: en.tensor_tensor(
                w_all[:, tb_ * 32:(tb_ + 1) * 32], prout[:, 96:128],
                routc[:, 2:3].broadcast_to([128, 32]), ALU.mult))(tb),
                reads=[rprout, rroutc], writes=[r_wall])
        P.op("sync", lambda en: en.dma_start(
            xn_d.ap().rearrange("p a b -> p (a b)"), xn_fm[:, :]),
            reads=[r_xnfm], writes=[rxnd], dma=True)
        p0.close()
        P.fence()

        rf = ExitStack()
        w_fm = sbt(rf, "w_fm", [32, S])
        rwfm = R("w_fm")
        posxx = sbt(rf, "posxx", [32, S])
        rposxx = R("posxx")
        povf = sbt(rf, "povf", [32, S])
        rpovf = R("povf")
        posb_tm = sbt(rf, "posb_tm", [128, NTB * 32])
        rposb = R("posb")
        eqr = sbt(rf, "eqr", [128, 32])
        reqr = R("eqr")
        epb = sbt(rf, "epb", [128, 32])
        repb = R("epb")
        m1r = sbt(rf, "m1r", [128, 32])
        rm1r = R("m1r")
        fsf = sbt(rf, "fsf", [128, 2])
        rfsf = R("fsf")
        for tb in range(NTB):
            bank = 2 + tb % 2
            P.op("tensor", (lambda tb_, b_: lambda en: en.transpose(
                pb[b_][0:32, 0:128], w_all[:, tb_ * 32:(tb_ + 1) * 32],
                identf[:, :]))(tb, bank),
                reads=[r_wall], writes=[pbr[bank]])
            P.op("vector", (lambda tb_, b_: lambda en: en.tensor_copy(
                w_fm[:, tb_ * 128:(tb_ + 1) * 128], pb[b_][0:32, 0:128]))(
                    tb, bank),
                reads=[pbr[bank]], writes=[rwfm])
        P.op("vector", lambda en: en.tensor_scalar(
            w_fm[:, :] if False else posxx[:, :], w_fm[:, :], 0.0, None,
            ALU.is_gt), reads=[rwfm], writes=[rposxx])
        P.op("vector", lambda en: en.tensor_tensor_scan(
            posxx[:, :], ones32[:, :], posxx[:, :], zero32[:, :],
            ALU.mult, ALU.add), reads=[rposxx], writes=[rposxx])
        P.op("vector", lambda en: en.tensor_scalar(
            posxx[:, :], posxx[:, :], -1.0, None, ALU.add),
            reads=[rposxx], writes=[rposxx])
        P.op("vector", lambda en: en.tensor_scalar(
            povf[:, :], posxx[:, :], CAP - 0.5, None, ALU.is_ge),
            reads=[rposxx], writes=[rpovf])
        P.op("vector", lambda en: en.scalar_tensor_tensor(
            posxx[:, :], povf[:, :], 1e9, posxx[:, :], ALU.mult, ALU.add),
            reads=[rpovf, rposxx], writes=[rposxx])
        for tb in range(NTB):
            bank = 2 + tb % 2
            P.op("tensor", (lambda tb_, b_: lambda en: en.transpose(
                pb[b_][0:128, 0:32], posxx[:, tb_ * 128:(tb_ + 1) * 128],
                identf[0:32, 0:32]))(tb, bank),
                reads=[rposxx], writes=[pbr[bank]])
            P.op("vector", (lambda tb_, b_: lambda en: en.tensor_copy(
                posb_tm[:, tb_ * 32:(tb_ + 1) * 32], pb[b_][0:128, 0:32]))(
                    tb, bank),
                reads=[pbr[bank]], writes=[rposb])
        for tb in range(NTB):
            P.op("vector", (lambda tb_: lambda en: en.tensor_tensor(
                epb[:, :], posb_tm[:, tb_ * 32:(tb_ + 1) * 32], ecr_sb[:, :],
                ALU.add))(tb),
                reads=[rposb, recr], writes=[repb])
            for r4 in range(4):
                P.op("vector", (lambda tb_, r_: lambda en: en.tensor_tensor(
                    eqr[:, :], erow_sb[:, :],
                    eid_sb[:, tb_ * 4 + r_:tb_ * 4 + r_ + 1]
                    .broadcast_to([128, 32]), ALU.is_equal))(tb, r4),
                    reads=[rerow, reid], writes=[reqr])
                P.op("vector", lambda en: en.tensor_tensor(
                    m1r[:, :], eqr[:, :], epb[:, :], ALU.mult),
                    reads=[reqr, repb], writes=[rm1r])
                P.op("vector", lambda en: en.reduce_sum(
                    fsf[:, 0:1], m1r[:, :], AX.X),
                    reads=[rm1r], writes=[rfsf], sync_after=True)
                P.op("vector", (lambda tb_, r_: lambda en: en.tensor_scalar(
                    gidx_sb[:, tb_ * 4 + r_:tb_ * 4 + r_ + 1], fsf[:, 0:1],
                    12288.0, 0.0, ALU.min, ALU.max))(tb, r4),
                    reads=[rfsf], writes=[rgidx])
                P.op("vector", (lambda tb_: lambda en: en.tensor_tensor(
                    m1r[:, :], eqr[:, :], w_all[:, tb_ * 32:(tb_ + 1) * 32],
                    ALU.mult))(tb),
                    reads=[reqr, r_wall], writes=[rm1r])
                P.op("vector", (lambda tb_, r_: lambda en: en.reduce_sum(
                    wr_sb[:, tb_ * 4 + r_:tb_ * 4 + r_ + 1], m1r[:, :],
                    AX.X))(tb, r4),
                    reads=[rm1r], writes=[rwr])
                P.op("gpsimd", (lambda tb_, r_: lambda en: en.indirect_dma_start(
                    idxg_d.ap(),
                    IndirectOffsetOnAxis(gidx_sb[:, tb_ * 4 + r_:
                                                 tb_ * 4 + r_ + 1], 0),
                    tokid_sb[:, tb_:tb_ + 1], None))(tb, r4),
                    reads=[rgidx, rtokid], writes=[ridxg], dma=True)
        rf.close()

        p1 = ExitStack()
        wipb = [sbt(p1, "wipb0", [128, KS * 128], BF16),
                sbt(p1, "wipb1", [128, KS * 128], BF16)]
        rwipb = [R("wipb0"), R("wipb1")]
        strip = [sbt(p1, "strip0", [128, S], BF16), sbt(p1, "strip1", [128, S], BF16)]
        rstrip = [R("strip0"), R("strip1")]
        woutc = [sbt(p1, "woutc0", [128, H], BF16), sbt(p1, "woutc1", [128, H], BF16)]
        rwoutc = [R("woutc0"), R("woutc1")]
        dtw = sbt(p1, "dtw", [32, S])
        rdtw = R("dtw")
        racrow = R("acrow")

        jorder = [66] + list(range(66))
        for it, jb in enumerate(jorder):
            wb = wipb[it % 2]; rwb = rwipb[it % 2]
            stp = strip[it % 2]; rstp = rstrip[it % 2]
            P.op("gpsimd", (lambda jb_, wb_: lambda en: en.dma_start(
                wb_[:, :].rearrange("p (a b) -> p a b", a=KS),
                wip_d.ap()[jb_].transpose([1, 0, 2])))(jb, wb),
                writes=[rwb], dma=True)
            for tt in range(4):
                bank = tt % 2
                mmt = None
                for ksl in range(KS):
                    mmt = P.op("tensor", (lambda k_, tt_, b_, wb_: lambda en:
                         en.matmul(
                             pb[b_][:, :],
                             wb_[:, k_ * 128:(k_ + 1) * 128],
                             xn_fm[:, k_ * 2048 + tt_ * 512:
                                   k_ * 2048 + (tt_ + 1) * 512],
                             start=(k_ == 0), stop=(k_ == KS - 1)))(
                                 ksl, tt, bank, wb),
                        reads=[rwb, r_xnfm] if ksl == 0 else [rwb],
                        writes=[pbr[bank]] if ksl == 0 else (),
                        deps=() if mmt is None else [mmt])
                pbr[bank].w = mmt
                if jb == 66:
                    P.op("scalar", (lambda tt_, b_: lambda en: en.copy(
                        dtw[:, tt_ * 512:(tt_ + 1) * 512], pb[b_][0:32, :]))(
                            tt, bank),
                        reads=[pbr[bank]], writes=[rdtw])
                elif jb < 32:
                    P.op("scalar", (lambda tt_, b_, stp_: lambda en: en.activation(
                        stp_[:, tt_ * 512:(tt_ + 1) * 512], pb[b_][:, :],
                        AF.Silu))(tt, bank, stp),
                        reads=[pbr[bank]], writes=[rstp])
                else:
                    P.op("scalar", (lambda tt_, b_, stp_: lambda en: en.copy(
                        stp_[:, tt_ * 512:(tt_ + 1) * 512], pb[b_][:, :]))(
                            tt, bank, stp),
                        reads=[pbr[bank]], writes=[rstp])
            if jb < 32:
                P.op("sync", (lambda jb_, stp_: lambda en: en.dma_start(
                    gate_d.ap()[:, jb_, :], stp_[:, :]))(jb, stp),
                    reads=[rstp], writes=[rgated] if jb == 31 else (), dma=True)
            elif jb < 66:
                P.op("sync", (lambda jb_, stp_: lambda en: en.dma_start(
                    hbc_d.ap()[:, jb_ - 32, 3:S + 3], stp_[:, :]))(jb, stp),
                    reads=[rstp], writes=[rhbcd] if jb == 65 else (), dma=True)
            else:
                P.op("vector", lambda en: en.tensor_scalar(
                    dtw[:, :], dtw[:, :], vecs_sb[:, 1:2], None, ALU.add),
                    reads=[rdtw], writes=[rdtw])
                P.op("scalar", lambda en: en.activation(
                    dt_fm[:, :], dtw[:, :], AF.Abs),
                    reads=[rdtw], writes=[r_dt])
                P.op("scalar", lambda en: en.activation(
                    dt_fm[:, :], dt_fm[:, :], AF.Exp, scale=-1.0),
                    reads=[r_dt], writes=[r_dt])
                P.op("scalar", lambda en: en.activation(
                    dt_fm[:, :], dt_fm[:, :], AF.Ln, bias=1.0),
                    reads=[r_dt], writes=[r_dt])
                P.op("vector", lambda en: en.tensor_scalar(
                    dtw[:, :], dtw[:, :], 0.0, None, ALU.max),
                    reads=[rdtw], writes=[rdtw])
                P.op("vector", lambda en: en.tensor_tensor(
                    dt_fm[:, :], dt_fm[:, :], dtw[:, :], ALU.add),
                    reads=[rdtw, r_dt], writes=[r_dt])
                P.op("vector", lambda en: en.tensor_scalar(
                    dt_fm[:, :], dt_fm[:, :], 100.0, None, ALU.min),
                    reads=[r_dt], writes=[r_dt])
                P.op("vector", lambda en: en.tensor_scalar(
                    dtw[:, :], dt_fm[:, :], acol_sb[:, :], None, ALU.mult),
                    reads=[r_dt, r_acol], writes=[rdtw])
                for c in range(NCH):
                    P.op("vector", (lambda c_: lambda en: en.tensor_tensor_scan(
                        acum_fm[:, c_ * 256:(c_ + 1) * 256], ones32[:, 0:256],
                        dtw[:, c_ * 256:(c_ + 1) * 256], zero32[:, :],
                        ALU.mult, ALU.add))(c),
                        reads=[rdtw], writes=[r_dt])
                P.op("sync", lambda en: en.dma_start(
                    acrow_d.ap().transpose([1, 0, 2]),
                    acum_fm[:, :].rearrange("h (c l) -> h c l", c=NCH)),
                    reads=[r_dt], writes=[racrow], dma=True)
        for tb in range(NTB):
            for wi, (src, dst) in enumerate(((dt_fm, dt_tm), (acum_fm, acum_tm))):
                bank = 2 + (tb * 2 + wi) % 2
                P.op("tensor", (lambda tb_, s_, b_: lambda en: en.transpose(
                    pb[b_][0:128, 0:32], s_[:, tb_ * 128:(tb_ + 1) * 128],
                    identf[0:32, 0:32]))(tb, src, bank),
                    reads=[r_dt], writes=[pbr[bank]])
                P.op("vector", (lambda tb_, d_, b_: lambda en:
                     en.tensor_copy(d_[:, tb_ * 32:(tb_ + 1) * 32],
                                    pb[b_][0:128, 0:32]))(tb, dst, bank),
                    reads=[pbr[bank]], writes=[r_dt])
        for sl in range(32):
            wc = woutc[sl % 2]; rwc = rwoutc[sl % 2]
            P.op("gpsimd", (lambda sl_, wc_: lambda en: en.dma_start(
                wc_[:, :], wout_d.ap()[sl_]))(sl, wc),
                writes=[rwc], dma=True)
            P.op("vector", (lambda sl_, wc_: lambda en: en.tensor_scalar(
                wc_[:, :], wc_[:, :], nw_sb[:, sl_:sl_ + 1], None, ALU.mult))(
                    sl, wc),
                reads=[rwc], writes=[rwc])
            P.op("sync", (lambda sl_, wc_: lambda en: en.dma_start(
                woutb_d.ap()[:, sl_, :], wc_[:, :]))(sl, wc),
                reads=[rwc], writes=[rwoutb] if sl == 31 else (), dma=True)
        p1.close()
        p01.close()
        P.fence()

        p2 = ExitStack()
        hbc_ch = sbt(p2, "hbc_ch", [128, 34 * 259], BF16)
        rhbc = R("hbc_ch")
        cvt = sbt(p2, "cvt", [128, 256])
        rcvt = R("cvt")
        sg_ch = [sbt(p2, "sg_ch0", [128, 32 * 256], BF16)]
        rsg = [R("sg0")]
        xs_tm = [sbt(p2, "xs_tm0", [128, INTER], BF16),
                 sbt(p2, "xs_tm1", [128, INTER], BF16)]
        rxs = [R("xs0"), R("xs1")]
        xdt_a = [sbt(p2, "xdt0", [128, INTER], BF16),
                 sbt(p2, "xdt1", [128, INTER], BF16)]
        rxdt = [R("xdt0"), R("xdt1")]
        btm = sbt(p2, "btm", [128, 256], BF16)
        rbtm = R("btm")
        gtm_sb = sbt(p2, "gtm_sb", [128, 512], BF16)
        rgtm = R("gtm")
        ltu = [sbt(p2, f"ltu{i}", [128, 512]) for i in range(2)]
        rltu = [R("ltu0"), R("ltu1")]
        ltb = [sbt(p2, f"ltb{i}", [128, 512], BF16) for i in range(2)]
        rltb = [R("ltb0"), R("ltb1")]
        mt_sb = [sbt(p2, f"mt_sb{i}", [128, 512], BF16) for i in range(2)]
        rmt = [R("mt0"), R("mt1")]
        xdd = [sbt(p2, f"xdd{i}", [128, 256], BF16) for i in range(2)]
        rxdd = [R("xdd0"), R("xdd1")]
        t1_sb = [sbt(p2, f"t1_sb{i}", [128, 256]) for i in range(2)]
        rt1 = [R("t10"), R("t11")]
        tsum_s = sbt(p2, "tsum_s", [128, 256])
        tsum = [tsum_s, tsum_s]
        rtsum_s = R("ts")
        rtsum = [rtsum_s, rtsum_s]
        ybf = [sbt(p2, f"ybf{i}", [128, 256], BF16) for i in range(2)]
        rybf = [R("ybf0"), R("ybf1")]
        st_f = sbt(p2, "st_f", [128, HEADS * 128])
        rst = [R(f"st{h}") for h in range(HEADS)]
        st_bf = sbt(p2, "st_bf", [128, HEADS * 128], BF16)
        rstbf = [R(f"stbf{h}") for h in range(HEADS)]
        sdec = sbt(p2, "sdec", [128, 128])
        rsdec = R("sdec")
        y_fm = sbt(p2, "y_fm", [128, 32 * 256], BF16)
        ryfm = [R(f"yfm{h}") for h in range(HEADS)]
        yg2 = [sbt(p2, "yg20", [128, 256], BF16), sbt(p2, "yg21", [128, 256], BF16)]
        ryg2 = [R("yg20"), R("yg21")]
        ds_fm = sbt(p2, "ds_fm", [32, 256])
        rdsfm = R("dsfm")
        ds_tm = sbt(p2, "ds_tm", [128, 64])
        rdstm = R("dstm")
        sdx_tm = sbt(p2, "sdx_tm", [128, 64])
        rsdx = R("sdx")
        ecs = sbt(p2, "ecs", [32, 1])
        recs = R("ecs")
        ecsr = sbt(p2, "ecsr", [1, 32])
        recsr = R("ecsr")
        dbc = sbt(p2, "dbc", [128, 32])
        rdbc = R("dbc")
        dcol = sbt(p2, "dcol", [128, 32])
        rdcol = R("dcol")
        xsd = [sbt(p2, f"xsd{i}", [128, 256]) for i in range(2)]
        rxsd = [R("xsd0"), R("xsd1")]
        rowbuf = [sbt(p2, f"rowbuf{i}", [1, 512]) for i in range(2)]
        rrowb = [R("rowb0"), R("rowb1")]
        resst_t = tsum_s
        rresq = rtsum_s
        wo_t = [sbt(p2, "wo_t0", [128, 32 * 256], BF16),
                sbt(p2, "wo_t1", [128, 32 * 256], BF16)]
        rwot = [R("wot0"), R("wot1")]
        var_sb = sbt(p2, "var_sb", [128, 4])
        rvar = R("var")
        rstd_tm = sbt(p2, "rstd_tm", [128, 2])
        rrstd = R("rstd")
        xres = [sbt(p2, "xres0", [128, H]), sbt(p2, "xres1", [128, H])]
        rxres = [R("xres0"), R("xres1")]

        P.op("tensor", lambda en: en.transpose(
            pb[2][0:1, 0:32], vecs_sb[:, 2:3], identf[0:32, 0:32]),
            writes=[pbr[2]])
        P.op("vector", lambda en: en.tensor_copy(ecsr[:, :], pb[2][0:1, 0:32]),
             reads=[pbr[2]], writes=[recsr])
        P.op("tensor", lambda en: en.matmul(
            pb[3][:, 0:32], ones1_sb[:, :], ecsr[:, :], start=True, stop=True),
            reads=[recsr], writes=[pbr[3]])
        P.op("vector", lambda en: en.tensor_copy(dcol[:, :], pb[3][:, 0:32]),
             reads=[pbr[3]], writes=[rdcol])

        for c in range(NCH):
            sgb = sg_ch[0]; rsgb = rsg[0]
            P.op("sync", (lambda c_: lambda en: en.dma_start(
                hbc_ch[:, :].rearrange("p (a b) -> p a b", a=34),
                hbc_d.ap()[:, :, c_ * 256:c_ * 256 + 259]))(c),
                reads=[rhbcd], writes=[rhbc], dma=True)
            P.op("sync", (lambda c_, sgb_: lambda en: en.dma_start(
                sgb_[:, :].rearrange("p (a b) -> p a b", a=32),
                gate_d.ap()[:, :, c_ * 256:(c_ + 1) * 256]))(c, sgb),
                reads=[rgated], writes=[rsgb], dma=True)
            for tb2 in range(2):
                P.op("sync", (lambda c_, t_: lambda en: en.dma_start(
                    xres[t_][:, :],
                    x_d.ap()[c_ * 256 + t_ * 128:c_ * 256 + (t_ + 1) * 128, :]))(
                        c, tb2),
                    writes=[rxres[tb2]], dma=True)
            for sl in [32, 33] + list(range(32)):
                P.op("vector", (lambda sl_: lambda en: en.tensor_scalar(
                    cvt[:, :], hbc_ch[:, sl_ * 259:sl_ * 259 + 256],
                    convw_sb[:, sl_ * 4:sl_ * 4 + 1], None, ALU.mult))(sl),
                    reads=[rhbc], writes=[rcvt])
                for k in range(1, 4):
                    P.op("vector", (lambda sl_, k_: lambda en:
                         en.scalar_tensor_tensor(
                             cvt[:, :],
                             hbc_ch[:, sl_ * 259 + k_:sl_ * 259 + k_ + 256],
                             convw_sb[:, sl_ * 4 + k_:sl_ * 4 + k_ + 1],
                             cvt[:, :], ALU.mult, ALU.add))(sl, k),
                        reads=[rhbc], writes=[rcvt])
                P.op("scalar", (lambda sl_: lambda en: en.activation(
                    hbc_ch[:, sl_ * 259:sl_ * 259 + 256], cvt[:, :], AF.Silu,
                    bias=convb_sb[:, sl_:sl_ + 1]))(sl),
                    reads=[rcvt], writes=[rhbc])
            gtok = None
            for sbk in range(2):
                gtok = P.op("tensor", (lambda s_: lambda en: en.matmul(
                    pb[1][:, s_ * 256:(s_ + 1) * 256],
                    hbc_ch[:, 32 * 259 + s_ * 128:32 * 259 + (s_ + 1) * 128],
                    hbc_ch[:, 33 * 259:33 * 259 + 256], start=True, stop=True))(sbk),
                    reads=[rhbc], writes=[pbr[1]] if sbk == 0 else (),
                    deps=() if gtok is None else [gtok])
            pbr[1].w = gtok
            P.op("vector", lambda en: en.scalar_tensor_tensor(
                gtm_sb[:, :], pb[1][:, :], 1.0, tri_sb[:, :], ALU.mult, ALU.mult),
                reads=[pbr[1]], writes=[rgtm])
            for sl in range(32):
                for hf in range(2):
                    bank = 6 + (sl * 2 + hf) % 2
                    P.op("tensor", (lambda sl_, h_, b_: lambda en: en.transpose(
                        pb[b_].ap().bitcast(BF16)[:, 0:128],
                        hbc_ch[:, sl_ * 259 + h_ * 128:sl_ * 259 + (h_ + 1) * 128],
                        identb[:, :]))(sl, hf, bank),
                        reads=[rhbc], writes=[pbr[bank]])
                    P.op("vector", (lambda sl_, h_, b_: lambda en: en.tensor_copy(
                        xs_tm[h_][:, sl_ * 128:(sl_ + 1) * 128],
                        pb[b_].ap().bitcast(BF16)[:, 0:128]))(sl, hf, bank),
                        reads=[pbr[bank]], writes=[rxs[hf]])
            for hf in range(2):
                bank = 6 + hf
                P.op("tensor", (lambda h_, b_: lambda en: en.transpose(
                    pb[b_].ap().bitcast(BF16)[:, 0:128],
                    hbc_ch[:, 32 * 259 + h_ * 128:32 * 259 + (h_ + 1) * 128],
                    identb[:, :]))(hf, bank),
                    reads=[rhbc], writes=[pbr[bank]])
                P.op("vector", (lambda h_, b_: lambda en: en.tensor_copy(
                    btm[:, h_ * 128:(h_ + 1) * 128],
                    pb[b_].ap().bitcast(BF16)[:, 0:128]))(hf, bank),
                    reads=[pbr[bank]], writes=[rbtm])
            P.op("scalar", (lambda c_: lambda en: en.activation(
                ds_fm[:, :], acum_fm[:, c_ * 256:(c_ + 1) * 256], AF.Exp,
                scale=-1.0, bias=acum_fm[:, c_ * 256 + 255:c_ * 256 + 256]))(c),
                reads=[r_dt], writes=[rdsfm])
            for hf in range(2):
                bank = 6 + hf
                P.op("tensor", (lambda h_, b_: lambda en: en.transpose(
                    pb[b_][0:128, 0:32], ds_fm[:, h_ * 128:(h_ + 1) * 128],
                    identf[0:32, 0:32]))(hf, bank),
                    reads=[rdsfm], writes=[pbr[bank]])
                P.op("vector", (lambda h_, b_: lambda en: en.tensor_copy(
                    ds_tm[:, h_ * 32:(h_ + 1) * 32], pb[b_][0:128, 0:32]))(
                        hf, bank),
                    reads=[pbr[bank]], writes=[rdstm])
            P.op("scalar", (lambda c_: lambda en: en.activation(
                sdx_tm[:, 0:64], acum_tm[:, c_ * 64:(c_ + 1) * 64], AF.Exp))(c),
                reads=[r_dt], writes=[rsdx])
            for hf in range(2):
                P.op("vector", (lambda c_, h_: lambda en: en.tensor_tensor(
                    xdt_a[h_][:, :].rearrange("p (a b) -> p a b", a=32),
                    xs_tm[h_][:, :].rearrange("p (a b) -> p a b", a=32),
                    dt_tm[:, (2 * c_ + h_) * 32:(2 * c_ + h_ + 1) * 32]
                    .unsqueeze(2).broadcast_to([128, 32, 128]),
                    ALU.mult))(c, hf),
                    reads=[rxs[hf], r_dt], writes=[rxdt[hf]])
            if c > 0:
                P.op("scalar", (lambda c_: lambda en: en.activation(
                    ecs[:, :], acum_fm[:, c_ * 256 + 255:c_ * 256 + 256], AF.Exp))(c),
                    reads=[r_dt], writes=[recs])
                P.op("tensor", lambda en: en.transpose(
                    pb[0][0:1, 0:32], ecs[:, :], identf[0:32, 0:32]),
                    reads=[recs], writes=[pbr[0]])
                P.op("vector", lambda en: en.tensor_copy(
                    ecsr[:, :], pb[0][0:1, 0:32]),
                    reads=[pbr[0]], writes=[recsr])
                P.op("tensor", lambda en: en.matmul(
                    pb[0][:, 0:32], ones1_sb[:, :], ecsr[:, :], start=True,
                    stop=True), reads=[recsr], writes=[pbr[0]])
                P.op("vector", lambda en: en.tensor_copy(dbc[:, :],
                                                         pb[0][:, 0:32]),
                     reads=[pbr[0]], writes=[rdbc])

            def row_dma(hg):
                P.op("sync", (lambda c_, hg_: lambda en: en.dma_start(
                    rowbuf[hg_ % 2][:, :],
                    acrow_d.ap()[c_, hg_ * 2:(hg_ + 1) * 2, :]
                    .rearrange("a b -> (a b)").unsqueeze(0)))(c, hg),
                    reads=[racrow], writes=[rrowb[hg % 2]], dma=True)
            row_dma(0)
            row_dma(1)
            for h in range(HEADS):
                par = h % 2
                bA = 3 * par
                bB = 1 + 3 * par
                bC = 2 + 3 * par
                hg = h // 2
                P.op("tensor", (lambda h_, b_, hg_: lambda en: en.matmul(
                    pb[b_][:, 0:256], ones1_sb[:, :],
                    rowbuf[hg_ % 2][0:1, (h_ % 2) * 256:(h_ % 2 + 1) * 256],
                    start=True, stop=True))(h, bA, hg),
                    reads=[rrowb[hg % 2]], writes=[pbr[bA]])
                for sbk in range(2):
                    P.op("scalar", (lambda c_, h_, s_, b_: lambda en: en.activation(
                        ltu[h_ % 2][:, s_ * 256:(s_ + 1) * 256], pb[b_][:, 0:256],
                        AF.Relu, scale=-1.0,
                        bias=acum_tm[:, (2 * c_ + s_) * 32 + h_:
                                     (2 * c_ + s_) * 32 + h_ + 1]))(c, h, sbk, bA),
                        reads=[pbr[bA], r_dt],
                        writes=[rltu[h % 2]] if sbk == 1 else (),
                        deps=())
                P.op("scalar", (lambda h_: lambda en: en.activation(
                    ltb[h_ % 2][:, :], ltu[h_ % 2][:, :], AF.Exp, scale=-1.0))(h),
                    reads=[rltu[h % 2]], writes=[rltb[h % 2]])
                P.op("vector", (lambda h_: lambda en: en.tensor_tensor(
                    mt_sb[h_ % 2][:, :], ltb[h_ % 2][:, :], gtm_sb[:, :],
                    ALU.mult))(h),
                    reads=[rltb[h % 2], rgtm], writes=[rmt[h % 2]])
                mmt = None
                for lh in range(2):
                    for sbk in range(2):
                        mmt = P.op("tensor", (lambda h_, lh_, s_, b_: lambda en:
                             en.matmul(
                                 pb[b_][:, lh_ * 128:(lh_ + 1) * 128],
                                 mt_sb[h_ % 2][:, s_ * 256 + lh_ * 128:
                                               s_ * 256 + (lh_ + 1) * 128],
                                 xdt_a[s_][:, h_ * 128:(h_ + 1) * 128],
                                 start=(s_ == 0), stop=(s_ == 1)))(h, lh, sbk, bB),
                            reads=[rmt[h % 2], rxdt[sbk]],
                            writes=[pbr[bB]] if (lh == 0 and sbk == 0) else (),
                            deps=() if mmt is None else [mmt])
                for sbk in range(2):
                    P.op("vector", (lambda h_, s_: lambda en: en.tensor_tensor(
                        xdd[h_ % 2][:, s_ * 128:(s_ + 1) * 128],
                        xdt_a[s_][:, h_ * 128:(h_ + 1) * 128],
                        ds_tm[:, s_ * 32 + h_:s_ * 32 + h_ + 1]
                        .broadcast_to([128, 128]), ALU.mult))(h, sbk),
                        reads=[rxdt[sbk], rdstm],
                        writes=[rxdd[h % 2]] if sbk == 1 else (),
                        deps=())
                sct = None
                for sbk in range(2):
                    sct = P.op("tensor", (lambda h_, s_, b_: lambda en: en.matmul(
                        pb[b_][:, 256:384], btm[:, s_ * 128:(s_ + 1) * 128],
                        xdd[h_ % 2][:, s_ * 128:(s_ + 1) * 128],
                        start=(s_ == 0), stop=(s_ == 1)))(h, sbk, bB),
                        reads=[rbtm, rxdd[h % 2]],
                        deps=() if sct is None else [sct])
                pbr[bB].w = mmt if sct is None else sct
                if c > 0:
                    yot = None
                    for lh in range(2):
                        yot = P.op("tensor", (lambda h_, lh_, b_: lambda en:
                             en.matmul(
                                 pb[b_][:, lh_ * 128:(lh_ + 1) * 128],
                                 hbc_ch[:, 33 * 259 + lh_ * 128:
                                        33 * 259 + (lh_ + 1) * 128],
                                 st_bf[:, h_ * 128:(h_ + 1) * 128],
                                 start=True, stop=True))(h, lh, bC),
                            reads=[rhbc, rstbf[h]],
                            writes=[pbr[bC]] if lh == 0 else (),
                            deps=() if yot is None else [yot])
                    pbr[bC].w = yot
                    for lh in range(2):
                        P.op("scalar", (lambda h_, lh_, b_: lambda en:
                             en.activation(
                                 t1_sb[h_ % 2][:, lh_ * 128:(lh_ + 1) * 128],
                                 pb[b_][:, lh_ * 128:(lh_ + 1) * 128], AF.Identity,
                                 scale=sdx_tm[:, lh_ * 32 + h_:
                                              lh_ * 32 + h_ + 1]))(h, lh, bC),
                            reads=[pbr[bC], rsdx],
                            writes=[rt1[h % 2]] if lh == 1 else (),
                            deps=())
                for hf in range(2):
                    P.op("vector", (lambda h_, f_: lambda en: en.tensor_tensor(
                        xsd[h_ % 2][:, f_ * 128:(f_ + 1) * 128],
                        xs_tm[f_][:, h_ * 128:(h_ + 1) * 128],
                        dcol[:, h_:h_ + 1].broadcast_to([128, 128]),
                        ALU.mult))(h, hf),
                        reads=[rxs[hf], rdcol],
                        writes=[rxsd[h % 2]] if hf == 1 else (),
                        deps=())
                if c > 0:
                    P.op("vector", (lambda h_: lambda en: en.tensor_tensor(
                        tsum[h_ % 2][:, :], t1_sb[h_ % 2][:, :],
                        xsd[h_ % 2][:, :], ALU.add))(h),
                        reads=[rt1[h % 2], rxsd[h % 2]], writes=[rtsum[h % 2]])
                    src_ts = rtsum
                else:
                    src_ts = rxsd
                P.op("vector", (lambda h_, b_, st_: lambda en: en.tensor_tensor(
                    ybf[h_ % 2][:, :], pb[b_][:, 0:256],
                    (tsum if st_ else xsd)[h_ % 2][:, :], ALU.add))(
                        h, bB, c > 0),
                    reads=[pbr[bB], src_ts[h % 2]], writes=[rybf[h % 2]])
                for lh in range(2):
                    bank = 6 + lh
                    P.op("tensor", (lambda h_, lh_, b_: lambda en: en.transpose(
                        pb[b_].ap().bitcast(BF16)[:, 0:128],
                        ybf[h_ % 2][:, lh_ * 128:(lh_ + 1) * 128],
                        identb[:, :]))(h, lh, bank),
                        reads=[rybf[h % 2]], writes=[pbr[bank]])
                    P.op("vector", (lambda h_, lh_, b_, sgb_: lambda en:
                         en.tensor_tensor(
                             y_fm[:, h_ * 256 + lh_ * 128:
                                  h_ * 256 + (lh_ + 1) * 128],
                             pb[b_].ap().bitcast(BF16)[:, 0:128],
                             sgb_[:, h_ * 256 + lh_ * 128:
                                  h_ * 256 + (lh_ + 1) * 128], ALU.mult))(
                                 h, lh, bank, sgb),
                        reads=[pbr[bank], rsgb],
                        writes=[ryfm[h]] if lh == 1 else (),
                        deps=())
                if c > 0:
                    P.op("gpsimd", (lambda h_: lambda en: en.tensor_tensor(
                        sdec[:, :], st_f[:, h_ * 128:(h_ + 1) * 128],
                        dbc[:, h_:h_ + 1].broadcast_to([128, 128]),
                        ALU.mult))(h),
                        reads=[rst[h], rdbc], writes=[rsdec])
                    P.op("vector", (lambda h_, b_: lambda en: en.tensor_tensor(
                        st_f[:, h_ * 128:(h_ + 1) * 128], sdec[:, :],
                        pb[b_][:, 256:384], ALU.add))(h, bB),
                        reads=[rsdec, pbr[bB]], writes=[rst[h]])
                else:
                    P.op("vector", (lambda h_, b_: lambda en: en.tensor_copy(
                        st_f[:, h_ * 128:(h_ + 1) * 128], pb[b_][:, 256:384]))(
                            h, bB),
                        reads=[pbr[bB]], writes=[rst[h]])
                P.op("gpsimd", (lambda h_: lambda en: en.tensor_copy(
                    st_bf[:, h_ * 128:(h_ + 1) * 128],
                    st_f[:, h_ * 128:(h_ + 1) * 128]))(h),
                    reads=[rst[h]], writes=[rstbf[h]])
                if h % 2 == 1 and h // 2 + 2 < 16:
                    row_dma(h // 2 + 2)
            vmt = None
            for sl in range(32):
                P.op("scalar", (lambda sl_: lambda en: en.activation(
                    yg2[sl_ % 2][:, :], y_fm[:, sl_ * 256:(sl_ + 1) * 256],
                    AF.Square))(sl),
                    reads=[ryfm[sl]], writes=[ryg2[sl % 2]])
                for tb2 in range(2):
                    vmt = P.op("tensor", (lambda sl_, t_: lambda en: en.matmul(
                        pb[6][:, 400 + t_:401 + t_],
                        yg2[sl_ % 2][:, t_ * 128:(t_ + 1) * 128],
                        onescol_bf[:, :],
                        start=(sl_ == 0), stop=(sl_ == 31)))(sl, tb2),
                        reads=[ryg2[sl % 2]],
                        writes=[pbr[6]] if (sl == 0 and tb2 == 0) else (),
                        deps=() if vmt is None else [vmt])
            pbr[6].w = vmt
            P.op("scalar", lambda en: en.activation(
                var_sb[:, 0:2], pb[6][:, 400:402], AF.Ln, scale=1.0 / 4096,
                bias=EPS), reads=[pbr[6]], writes=[rvar])
            P.op("scalar", lambda en: en.activation(
                rstd_tm[:, 0:2], var_sb[:, 0:2], AF.Exp, scale=-0.5),
                reads=[rvar], writes=[rrstd])
            for jt in range(8):
                wob = wo_t[jt % 2]; rwob = rwot[jt % 2]
                P.op("sync", (lambda jt_, wob_: lambda en: en.dma_start(
                    wob_[:, :].rearrange("p (a b) -> p a b", a=32),
                    woutb_d.ap()[:, :, jt_ * 256:(jt_ + 1) * 256]))(jt, wob),
                    reads=[rwoutb], writes=[rwob], dma=True)
                for tb2 in range(2):
                    bank = 1 + 3 * (jt % 2)
                    mmo = None
                    for isl in range(32):
                        mmo = P.op("tensor", (lambda jt_, t_, i_, b_, wob_:
                             lambda en: en.matmul(
                                 pb[b_][:, 0:256],
                                 y_fm[:, i_ * 256 + t_ * 128:
                                      i_ * 256 + (t_ + 1) * 128],
                                 wob_[:, i_ * 256:(i_ + 1) * 256],
                                 start=(i_ == 0), stop=(i_ == 31)))(
                                     jt, tb2, isl, bank, wob),
                            reads=[ryfm[isl], rwob],
                            writes=[pbr[bank]] if isl == 0 else (),
                            deps=() if mmo is None else [mmo])
                    pbr[bank].w = mmo
                    P.op("vector", (lambda jt_, t_, b_: lambda en:
                         en.tensor_tensor(
                             resst_t[:, 0:256], pb[b_][:, 0:256],
                             rstd_tm[:, t_:t_ + 1].broadcast_to([128, 256]),
                             ALU.mult))(jt, tb2, bank),
                        reads=[pbr[bank], rrstd], writes=[rresq])
                    P.op("vector", (lambda jt_, t_: lambda en:
                         en.tensor_tensor(
                             xres[t_][:, jt_ * 256:(jt_ + 1) * 256],
                             xres[t_][:, jt_ * 256:(jt_ + 1) * 256],
                             resst_t[:, 0:256], ALU.add))(jt, tb2),
                        reads=[rresq],
                        writes=[rxres[tb2]] if jt == 7 else (),
                        deps=())
            for tb2 in range(2):
                P.op("sync", (lambda c_, t_: lambda en: en.dma_start(
                    res_d.ap()[:, c_ * 2 + t_, :], xres[t_][:, :]))(c, tb2),
                    reads=[rxres[tb2]], writes=[rresd[c * 2 + tb2]], dma=True)
        p2.close()
        p12.close()
        P.fence()

        p3 = ExitStack()
        gwT = [sbt(p3, f"gwT{i}", [128, KS * NI], BF16) for i in range(2)]
        rgw = [R("gw0"), R("gw1")]
        uwT = [sbt(p3, f"uwT{i}", [128, KS * NI], BF16) for i in range(2)]
        ruw = [R("uw0"), R("uw1")]
        dwT = [sbt(p3, f"dwT{i}", [128, NIB * H], BF16) for i in range(2)]
        rdw = [R("dw0"), R("dw1")]
        xg_tm = sbt(p3, "xg_tm", [128, 3 * H], BF16)
        rxg = R("xg_tm")
        xg_fm = sbt(p3, "xg_fm", [128, KS * CAP], BF16)
        rxgf = R("xg_fm")
        hfm = sbt(p3, "hfm", [128, NIB * CAP], BF16)
        rhfm = [R(f"hfm{i}") for i in range(NIB)]
        sgt = sbt(p3, "sgt", [128, CAP], BF16)
        rsgt = R("sgt")
        ubf = sbt(p3, "ubf", [128, CAP], BF16)
        rubf = R("ubf")
        ysb = [sbt(p3, f"ysb{i}", [128, H]) for i in range(2)]
        rysb = [R("ysb0"), R("ysb1")]
        idxcol = sbt(p3, "idxcol", [128, 3], I32)
        ridxc = R("idxcol")

        P.op("vector", lambda en: en.memset(xg_tm[:, :], 0.0), writes=[rxg])
        for e in range(NE):
            par = e % 2
            P.op("gpsimd", (lambda e_, p_: lambda en: en.dma_start(
                gwT[p_][:, :].rearrange("p (a b) -> p a b", a=KS),
                gw_d.ap()[e_].transpose([1, 0, 2])))(e, par),
                writes=[rgw[par]], dma=True)
            P.op("gpsimd", (lambda e_, p_: lambda en: en.dma_start(
                uwT[p_][:, :].rearrange("p (a b) -> p a b", a=KS),
                uw_d.ap()[e_].transpose([1, 0, 2])))(e, par),
                writes=[ruw[par]], dma=True)
            P.op("gpsimd", (lambda e_, p_: lambda en: en.dma_start(
                dwT[p_][:, :].rearrange("p (a b) -> p a b", a=NIB),
                dw_d.ap()[e_].transpose([1, 0, 2])))(e, par),
                writes=[rdw[par]], dma=True)
            for b in range(3):
                P.op("sync", (lambda e_, b_: lambda en: en.dma_start(
                    idxcol[:, b_:b_ + 1],
                    idxg_d.ap()[e_ * CAP + b_ * 128:e_ * CAP + (b_ + 1) * 128, :]))(
                        e, b),
                    reads=[ridxg], writes=[ridxc] if b == 2 else (), dma=True)
            for b in range(3):
                P.op("gpsimd", (lambda e_, b_: lambda en: en.indirect_dma_start(
                    xg_tm[:, b_ * H:(b_ + 1) * H], None, xntm_d.ap(),
                    IndirectOffsetOnAxis(idxcol[:, b_:b_ + 1], 0)))(e, b),
                    reads=[ridxc, rxntm_d], writes=[rxg] if b == 2 else (),
                    dma=True)
            for b in range(3):
                for ksl in range(KS):
                    bank = 6 + (b * KS + ksl) % 2
                    P.op("tensor", (lambda b_, k_, bk_: lambda en: en.transpose(
                        pb[bk_].ap().bitcast(BF16)[:, 0:128],
                        xg_tm[:, b_ * H + k_ * 128:b_ * H + (k_ + 1) * 128],
                        identb[:, :]))(b, ksl, bank),
                        reads=[rxg], writes=[pbr[bank]])
                    P.op("vector", (lambda b_, k_, bk_: lambda en: en.tensor_copy(
                        xg_fm[:, k_ * CAP + b_ * 128:k_ * CAP + (b_ + 1) * 128],
                        pb[bk_].ap().bitcast(BF16)[:, 0:128]))(b, ksl, bank),
                        reads=[pbr[bank]], writes=[rxgf])
            for ib in range(NIB):
                gmt = None
                for ksl in range(KS):
                    gmt = P.op("tensor", (lambda i_, k_, p_: lambda en: en.matmul(
                        pb[0][:, 0:CAP],
                        gwT[p_][:, k_ * NI + i_ * 128:k_ * NI + (i_ + 1) * 128],
                        xg_fm[:, k_ * CAP:(k_ + 1) * CAP],
                        start=(k_ == 0), stop=(k_ == KS - 1)))(ib, ksl, par),
                        reads=[rgw[par], rxgf],
                        writes=[pbr[0]] if ksl == 0 else (),
                        deps=() if gmt is None else [gmt])
                pbr[0].w = gmt
                umt = None
                for ksl in range(KS):
                    umt = P.op("tensor", (lambda i_, k_, p_: lambda en: en.matmul(
                        pb[1][:, 0:CAP],
                        uwT[p_][:, k_ * NI + i_ * 128:k_ * NI + (i_ + 1) * 128],
                        xg_fm[:, k_ * CAP:(k_ + 1) * CAP],
                        start=(k_ == 0), stop=(k_ == KS - 1)))(ib, ksl, par),
                        reads=[ruw[par], rxgf],
                        writes=[pbr[1]] if ksl == 0 else (),
                        deps=() if umt is None else [umt])
                pbr[1].w = umt
                P.op("scalar", lambda en: en.activation(
                    sgt[:, :], pb[0][:, 0:CAP], AF.Silu),
                    reads=[pbr[0]], writes=[rsgt])
                P.op("scalar", lambda en: en.copy(ubf[:, :], pb[1][:, 0:CAP]),
                     reads=[pbr[1]], writes=[rubf])
                P.op("vector", (lambda i_: lambda en: en.tensor_tensor(
                    hfm[:, i_ * CAP:(i_ + 1) * CAP], sgt[:, :], ubf[:, :],
                    ALU.mult))(ib),
                    reads=[rsgt, rubf], writes=[rhfm[ib]])
            for sb3 in range(3):
                ys = ysb[sb3 % 2]; rys = rysb[sb3 % 2]
                for jt in range(4):
                    bank = 4 + jt % 2
                    dnt = None
                    for ib in range(NIB):
                        dnt = P.op("tensor", (lambda s_, j_, i_, b_, p_:
                             lambda en: en.matmul(
                                 pb[b_][:, 0:512],
                                 hfm[:, i_ * CAP + s_ * 128:
                                     i_ * CAP + (s_ + 1) * 128],
                                 dwT[p_][:, i_ * H + j_ * 512:
                                         i_ * H + (j_ + 1) * 512],
                                 start=(i_ == 0), stop=(i_ == NIB - 1)))(
                                     sb3, jt, ib, bank, par),
                            reads=[rhfm[ib], rdw[par]],
                            writes=[pbr[bank]] if ib == 0 else (),
                            deps=() if dnt is None else [dnt])
                    pbr[bank].w = dnt
                    P.op("vector", (lambda s_, j_, b_, ys_: lambda en:
                         en.tensor_copy(
                             ys_[:, j_ * 512:(j_ + 1) * 512], pb[b_][:, 0:512]))(
                                 sb3, jt, bank, ys),
                        reads=[pbr[bank]],
                        writes=[rys] if jt == 3 else (),
                        deps=())
                P.op("sync", (lambda e_, s_, ys_: lambda en: en.dma_start(
                    ybuf_d.ap()[e_ * CAP + s_ * 128:e_ * CAP + (s_ + 1) * 128, :],
                    ys_[:, :]))(e, sb3, ys),
                    reads=[rys], writes=[rybuf] if (e == NE - 1 and sb3 == 2)
                    else (), dma=True)
        p3.close()

        pep = ExitStack()
        rtile = sbt(pep, "rtile", [128, H])
        rrt = R("rtile")
        gtmp = [sbt(pep, f"gtmp{i}", [128, H]) for i in range(2)]
        rgt = [R("gt0"), R("gt1")]
        for tb in range(NTB):
            P.op("sync", (lambda tb_: lambda en: en.dma_start(
                rtile[:, :], res_d.ap()[:, tb_, :]))(tb),
                reads=[rresd[tb]], writes=[rrt], dma=True)
            for r4 in range(4):
                gp = gtmp[r4 % 2]; rgp = rgt[r4 % 2]
                P.op("gpsimd", (lambda tb_, r_, g_: lambda en:
                     en.indirect_dma_start(
                         g_[:, :], None, ybuf_d.ap(),
                         IndirectOffsetOnAxis(
                             gidx_sb[:, tb_ * 4 + r_:tb_ * 4 + r_ + 1], 0)))(
                             tb, r4, gp),
                    reads=[rgidx, rybuf], writes=[rgp], dma=True)
                P.op("vector", (lambda tb_, r_, g_: lambda en:
                     en.scalar_tensor_tensor(
                         rtile[:, :], g_[:, :],
                         wr_sb[:, tb_ * 4 + r_:tb_ * 4 + r_ + 1],
                         rtile[:, :], ALU.mult, ALU.add))(tb, r4, gp),
                    reads=[rgp, rwr], writes=[rrt])
            P.op("sync", (lambda tb_: lambda en: en.dma_start(
                out_d.ap()[tb_ * 128:(tb_ + 1) * 128, :], rtile[:, :]))(tb),
                reads=[rrt], dma=True)
        if debug:
            P.fence()
            dbgt = sbt(pep, "dbgt", [128, 4096])
            P.op("vector", lambda en: en.tensor_copy(dbgt[:, 0:512], w_all[:, :]),
                 reads=[r_wall])
            P.op("vector", lambda en: en.tensor_copy(dbgt[:, 3584:3680],
                                                     ldbg[:, :]), reads=[rldbg])
            P.op("vector", lambda en: en.tensor_copy(
                dbgt[:, 3680:3744], gidx_sb[:, :]), reads=[rgidx])
            P.op("vector", lambda en: en.tensor_copy(
                dbgt[:, 3744:3808], wr_sb[:, :]), reads=[rwr])
            P.op("vector", lambda en: en.tensor_copy(
                dbgt[:, 3808:3872], eid_sb[:, :]), reads=[reid])
            P.op("sync", lambda en: en.dma_start(dbgv_d.ap(), dbgt[:, :]), dma=True)
        pep.close()

        blk = stack.enter_context(nc.Block("main"))
        P.emit(nc, blk, stack)
    return nc


def _prep(inputs):
    wip = inputs["L0_attn_in_proj_w"]
    wip_p = np.zeros((JPAD, H), np.float32)
    wip_p[:8480] = wip
    wip_b = np.ascontiguousarray(
        wip_p.reshape(NJB, 128, KS, 128).transpose(0, 2, 3, 1))
    wout_b = np.ascontiguousarray(
        inputs["L0_attn_out_proj_w"].T.reshape(32, 128, H))
    gw_b = np.ascontiguousarray(
        inputs["L0_ffn_gate_w"].transpose(0, 2, 1).reshape(NE, KS, 128, NI))
    uw_b = np.ascontiguousarray(
        inputs["L0_ffn_up_w"].transpose(0, 2, 1).reshape(NE, KS, 128, NI))
    dw_b = np.ascontiguousarray(
        inputs["L0_ffn_down_w"].transpose(0, 2, 1).reshape(NE, NIB, 128, H))
    rwt_b = np.ascontiguousarray(
        inputs["L0_ffn_router_w"].T.reshape(KS, 128, NE).transpose(1, 0, 2))
    convw = np.ascontiguousarray(
        inputs["L0_attn_conv_w"][:, 0, :].reshape(34, 128, 4).transpose(1, 0, 2))
    convb = np.ascontiguousarray(inputs["L0_attn_conv_b"].reshape(34, 128).T)
    nw = np.ascontiguousarray(inputs["L0_attn_norm_w"].reshape(32, 128).T)
    vecs = np.ascontiguousarray(np.stack([
        inputs["L0_attn_A_log"], inputs["L0_attn_dt_bias"], inputs["L0_attn_D"]],
        axis=1))
    consts = np.tile(np.asarray(CONST_VALS, np.float32)[None, :], (128, 1))
    tri = np.zeros((128, 512), np.float32)
    for sp in range(128):
        tri[sp, sp:256] = 1.0
        if 128 + sp < 256:
            tri[sp, 256 + 128 + sp:512] = 1.0
    ident = np.eye(128, dtype=np.float32)
    ones1 = np.ones((1, 128), np.float32)
    onescol = np.ones((128, 1), np.float32)
    tokid = np.arange(S, dtype=np.int32).reshape(NTB, 128).T.copy()
    erow = np.tile(np.arange(NE, dtype=np.float32)[None, :], (128, 1))
    ecr = erow * float(CAP)
    return {
        "wip": wip_b, "wout": wout_b, "gw": gw_b, "uw": uw_b, "dw": dw_b,
        "rwt": rwt_b, "convw": convw, "convb": convb, "nw": nw, "vecs": vecs,
        "consts": consts, "tri": tri, "ident": ident, "ones1": ones1,
        "onescol": onescol, "tokid": tokid, "erow": erow, "ecr": ecr,
    }


def kernel(**inputs):
    shared = _prep(inputs)
    x = inputs["L0_attn_hidden_states"]
    nc = build_nc(debug=False)
    in_maps = []
    for c in range(NCORES):
        m = dict(shared)
        m["x"] = np.ascontiguousarray(x[c])
        in_maps.append(m)
    res = run_bass_kernel_spmd(nc, in_maps, core_ids=list(range(NCORES)))
    return np.stack([np.asarray(r["out"]) for r in res.results], axis=0)
